# version 20
# baseline (speedup 1.0000x reference)
import numpy as np
import ml_dtypes
from contextlib import ExitStack
import concourse.bass as bass
import concourse.mybir as mybir
from concourse.bass_utils import run_bass_kernel_spmd

F32 = mybir.dt.float32
BF16 = mybir.dt.bfloat16
AF = mybir.ActivationFunctionType
ALU = mybir.AluOpType
AX = mybir.AxisListType.X

D = 1024
NHEAD = 16
HD = 64
NCH = 8
TOK = 2048
NT4 = TOK // 512
NPT = TOK // 128
LN_EPS = 1e-5
RMS_EPS = 1e-6
L0_SCALE = 0.125
L1_SCALE = 1.0 / 65536.0
NEXP = 4096
DCD = 4096


class Ctx:
    def __init__(self, nc, es):
        self.nc = nc
        self.es = es
        self.sem = {}
        self.cnt = {}
        for name in ["gio", "gst", "sy", "pe", "act", "dve", "pool"]:
            self._mk(name)

    def _mk(self, name):
        self.sem[name] = self.es.enter_context(self.nc.semaphore(name))
        self.cnt[name] = 0

    def inc(self, name, inst, k=1):
        if name not in self.sem:
            self._mk(name)
        self.cnt[name] += k
        inst.then_inc(self.sem[name], k)
        return self.cnt[name]

    def w(self, eng, name, val=None):
        if name not in self.sem:
            self._mk(name)
        v = self.cnt[name] if val is None else val
        if v > 0:
            eng.wait_ge(self.sem[name], v)


def build_program(upto=99, taps=(), dev_xmid=False, dev_expose=False):
    nc = bass.Bass("TRN2", target_bir_lowering=False)
    dt_ = {}

    def din(name, shape, dtype=F32):
        dt_[name] = nc.dram_tensor(name, shape, dtype, kind="ExternalInput").ap()

    def dou(name, shape, dtype=F32):
        dt_[name] = nc.dram_tensor(name, shape, dtype, kind="ExternalOutput").ap()

    def dit(name, shape, dtype):
        dt_[name] = nc.dram_tensor(name, shape, dtype, kind="Internal").ap()

    din("xT", [D, TOK])
    din("l0_wqkvT", [D, 3 * D]); din("l0_bqkv_col", [128, 24]); din("l0_bqkv_vrow", [1, D])
    din("l0_wprojT", [D, D]); din("l0_bproj_col", [128, 8])
    din("qpat_col", [128, 8]); din("kpat_col", [128, 8])
    din("l0_ffnT", [D, D]); din("alpha_col", [128, 1]); din("l0_fbias_col", [128, 8])
    din("l1_wqkvT", [D, 3 * D]); din("l1_woT", [D, D])
    din("wq_p", [D, 512]); din("keysP", [512, 64]); din("ktp", [512, NEXP])
    din("down_T", [D, NEXP]); din("up_nat", [NEXP, D])
    din("wg", [D, DCD]); din("wu", [D, DCD]); din("wd", [DCD, D])
    din("bg_col", [128, 32]); din("bu_col", [128, 32]); din("bd_col", [128, 8])
    din("blk2", [128, 2]); din("sel2", [2, 128]); din("epscol", [128, 4])
    (dou if (dev_xmid or dev_expose) else dit)("x_mid", [D, TOK], BF16)
    (dou if dev_expose else dit)("g_act", [D, TOK], BF16)
    (dou if dev_expose else dit)("sw_dram", [TOK, NEXP], BF16)
    (dou if dev_expose else dit)("g_dram", [TOK, NEXP], BF16)
    dou("outT", [D, TOK])
    for t in taps:
        shp = {"t_xn": [D, TOK], "t_qn": [D, TOK], "t_kn": [D, TOK], "t_v": [128, 1040 * NPT],
               "t_attn": [D, TOK], "t_x1": [D, TOK], "t_g": [D, TOK],
               "t_qpk": [512, TOK], "t_sims": [TOK, 512], "t_tau": [128, 64], "t_negc": [128, 64],
               "t_sw": [TOK, NEXP], "t_gg": [TOK, NEXP], "t_mlp": [D, TOK], "t_g1": [D, TOK],
               "t_xn1": [D, TOK], "t_attn1": [D, TOK]}[t]
        dou(t, shp)

    es = ExitStack()
    with es:
        E = es.enter_context
        A = E(nc.sbuf_tensor("A", [128, 8 * TOK], BF16))
        Bb = E(nc.sbuf_tensor("Bb", [128, 8 * TOK], BF16))
        C = E(nc.sbuf_tensor("C", [128, 8 * TOK], BF16))
        Dv = E(nc.sbuf_tensor("Dv", [128, 1040 * NPT], BF16))
        Eb = E(nc.sbuf_tensor("Eb", [128, 4 * TOK], BF16))
        Fb = E(nc.sbuf_tensor("Fb", [128, 4 * TOK], BF16))
        WB = E(nc.sbuf_tensor("WB", [128, 10240], BF16))
        SCR = E(nc.sbuf_tensor("SCR", [128, 4096], F32))
        SM = E(nc.sbuf_tensor("SM", [128, 1280], F32))
        ONESbr = E(nc.sbuf_tensor("ONESbr", [1, 128], BF16))
        ONESfr = E(nc.sbuf_tensor("ONESfr", [1, 128], F32))
        ONESfc = E(nc.sbuf_tensor("ONESfc", [128, 1], F32))
        ONESbc = E(nc.sbuf_tensor("ONESbc", [128, 1], BF16))
        CB = E(nc.sbuf_tensor("CB", [128, 80], F32))
        BGU = E(nc.sbuf_tensor("BGU", [128, 64], F32))
        SEL = E(nc.sbuf_tensor("SEL", [2, 128], F32))
        BLK2B = E(nc.sbuf_tensor("BLK2B", [128, 2], BF16))
        KEYSP = E(nc.sbuf_tensor("KEYSP", [128, 256], BF16))
        BROW = E(nc.sbuf_tensor("BROW", [1, D], BF16))
        psA = E(nc.psum_tensor("psA", [128, 2048], F32))
        psB = E(nc.psum_tensor("psB", [128, 2048], F32))
        blk = E(nc.Block())
        c = Ctx(nc, es)
        T_, S_, V_, G_, SY_ = blk.tensor, blk.scalar, blk.vector, blk.gpsimd, blk.sync

        def _(g):
            ii = [g.dma_start(out=CB[:, 0:24], in_=dt_["l0_bqkv_col"]),
                  g.dma_start(out=CB[:, 24:32], in_=dt_["l0_bproj_col"]),
                  g.dma_start(out=CB[:, 32:40], in_=dt_["qpat_col"]),
                  g.dma_start(out=CB[:, 40:48], in_=dt_["kpat_col"]),
                  g.dma_start(out=CB[:, 48:49], in_=dt_["alpha_col"]),
                  g.dma_start(out=CB[:, 49:57], in_=dt_["l0_fbias_col"]),
                  g.dma_start(out=CB[:, 57:65], in_=dt_["bd_col"]),
                  g.dma_start(out=CB[:, 65:67], in_=dt_["blk2"]),
                  g.dma_start(out=BLK2B[:], in_=dt_["blk2"]),
                  g.dma_start(out=CB[:, 67:71], in_=dt_["epscol"]),
                  g.dma_start(out=SEL[:], in_=dt_["sel2"]),
                  g.dma_start(out=KEYSP[:].rearrange("p (h k) -> p h k", h=4), in_=dt_["keysP"].rearrange("(h p) k -> p h k", p=128)),
                  g.dma_start(out=BROW[:], in_=dt_["l0_bqkv_vrow"]),
                  g.dma_start(out=BGU[:, 0:32], in_=dt_["bg_col"]),
                  g.dma_start(out=BGU[:, 32:64], in_=dt_["bu_col"])]
            for i in ii:
                c.inc("gio", i, 16)
            c.inc("pool", g.memset(ONESbr[:], 1.0))
            c.inc("pool", g.memset(ONESfr[:], 1.0))
            c.inc("pool", g.memset(ONESfc[:], 1.0))
            c.inc("pool", g.memset(ONESbc[:], 1.0))
        G_(_)
        CONST_GIO = c.cnt["gio"]
        CONST_POOL = c.cnt["pool"]

        def tapout(name, src_ap, rows=None, deps=()):
            if name not in taps:
                return
            def _(g):
                for nm, v in deps:
                    c.w(g, nm, v)
                i = g.dma_start(out=dt_[name] if rows is None else dt_[name][rows[0]:rows[1], :],
                                in_=src_ap)
                c.inc("gst", i, 16)
            G_(_)

        def emit_ln(x_src, prev_done=()):
            def gl(g):
                for nm, v_ in prev_done:
                    c.w(g, nm, v_)
                i = g.dma_start(out=A[:].rearrange("p (ch t) -> p ch t", ch=NCH),
                                in_=x_src.rearrange("(ch p) t -> p ch t", p=128))
                c.inc("gio", i, 16)
            G_(gl)
            LD = c.cnt["gio"]

            sqd = {}
            mmd = {}
            for ch in range(NCH):
                def s1(s, ch=ch):
                    if ch == 0:
                        c.w(s, "gio", LD)
                    if ch >= 2:
                        c.w(s, "pe", mmd[ch - 2])
                    i = s.activation(out=SCR[:, (ch % 2) * 2048:(ch % 2) * 2048 + 2048],
                                     in_=A[:, ch * TOK:(ch + 1) * TOK], func=AF.Square, bias=0.0, scale=1.0)
                    sqd[ch] = c.inc("act", i)
                S_(s1)

                def t1(t, ch=ch):
                    if ch == 0:
                        c.w(t, "gio", LD)
                        c.w(t, "pool", CONST_POOL)
                    for nt in range(NT4):
                        t.matmul(psB[0:1, nt * 512:(nt + 1) * 512], ONESbc[:],
                                 A[:, ch * TOK + nt * 512:ch * TOK + (nt + 1) * 512],
                                 start=(ch == 0), stop=(ch == NCH - 1))
                    c.w(t, "act", sqd[ch])
                    for nt in range(NT4):
                        i = t.matmul(psA[0:1, nt * 512:(nt + 1) * 512], ONESfc[:],
                                     SCR[:, (ch % 2) * 2048 + nt * 512:(ch % 2) * 2048 + (nt + 1) * 512],
                                     start=(ch == 0), stop=(ch == NCH - 1))
                    mmd[ch] = c.inc("pe", i)
                T_(t1)
            MM = c.cnt["pe"]

            def s2(s):
                c.w(s, "pe", MM)
                i = s.activation(out=SCR[0:1, 0:2048], in_=psB[0:1, :], func=AF.Identity, bias=0.0, scale=1.0 / D)
                c.inc("act", i)
                i = s.activation(out=SCR[64:65, 2048:4096], in_=psA[0:1, :], func=AF.Identity, bias=0.0, scale=1.0 / D)
                c.inc("act", i)
            S_(s2)
            A1 = c.cnt["act"]

            def v1(v):
                c.w(v, "act", A1)
                i = v.tensor_tensor(out=SCR[64:65, 0:2048], in0=SCR[0:1, 0:2048], in1=SCR[0:1, 0:2048], op=ALU.mult)
                k = c.inc("dve", i)
                c.w(v, "dve", k)
                i = v.tensor_tensor(out=SCR[32:33, 2048:4096], in0=SCR[64:65, 2048:4096], in1=SCR[64:65, 0:2048], op=ALU.subtract)
                c.inc("dve", i)
            V_(v1)
            D1 = c.cnt["dve"]

            def s3(s):
                c.w(s, "dve", D1)
                i = s.activation(out=SCR[32:33, 0:2048], in_=SCR[32:33, 2048:4096], func=AF.Ln, bias=CB[0:1, 67:68], scale=1.0)
                k = c.inc("act", i)
                c.w(s, "act", k)
                i = s.activation(out=SCR[0:1, 2048:4096], in_=SCR[32:33, 0:2048], func=AF.Exp, bias=0.0, scale=-0.5)
                c.inc("act", i)
            S_(s3)
            A2 = c.cnt["act"]

            def v2(v):
                c.w(v, "act", A2)
                i = v.tensor_tensor(out=SCR[32:33, 0:2048], in0=SCR[0:1, 0:2048], in1=SCR[0:1, 2048:4096], op=ALU.mult)
                c.inc("dve", i)
            V_(v2)
            D2a = c.cnt["dve"]

            def s3b(s):
                c.w(s, "dve", D2a)
                i = s.activation(out=SCR[0:1, 0:2048], in_=SCR[32:33, 0:2048], func=AF.Identity, bias=0.0, scale=1.0)
                c.inc("act", i)
            S_(s3b)
            A2b = c.cnt["act"]

            def v2b(v):
                c.w(v, "act", A2b)
                i = v.engine_nop() if False else None
            D2 = ("act", A2b)

            def t2(t):
                c.w(t, "act", A2b)
                for nt in range(NT4):
                    t.matmul(psA[:, nt * 512:(nt + 1) * 512], ONESfr[:], SCR[0:1, 2048 + nt * 512:2048 + (nt + 1) * 512],
                             start=True, stop=True)
                for nt in range(NT4):
                    i = t.matmul(psB[:, nt * 512:(nt + 1) * 512], ONESfr[:], SCR[0:1, nt * 512:(nt + 1) * 512],
                                 start=True, stop=True)
                c.inc("pe", i)
            T_(t2)
            BC = c.cnt["pe"]

            def s4(s):
                c.w(s, "pe", BC)
                i = s.activation(out=SCR[:, 0:2048], in_=psA[:], func=AF.Identity, bias=0.0, scale=1.0)
                c.inc("act", i)
                i = s.activation(out=SCR[:, 2048:4096], in_=psB[:], func=AF.Identity, bias=0.0, scale=1.0)
                c.inc("act", i)
            S_(s4)
            A3 = c.cnt["act"]

            def v3(v):
                c.w(v, "act", A3)
                for ch in range(NCH):
                    for hf in range(2):
                        pb = (ch * 2 + hf) % 2
                        c.w(v, "dve", c.cnt["dve"])
                        i = v.tensor_tensor(out=psA[:, pb * 1024:pb * 1024 + 1024],
                                            in0=A[:, ch * TOK + hf * 1024:ch * TOK + hf * 1024 + 1024],
                                            in1=SCR[:, hf * 1024:hf * 1024 + 1024], op=ALU.mult)
                        k = c.inc("dve", i)
                        c.w(v, "dve", k)
                        i = v.tensor_tensor(out=Bb[:, ch * TOK + hf * 1024:ch * TOK + hf * 1024 + 1024],
                                            in0=psA[:, pb * 1024:pb * 1024 + 1024],
                                            in1=SCR[:, 2048 + hf * 1024:2048 + hf * 1024 + 1024], op=ALU.subtract)
                        c.inc("dve", i)
            V_(v3)
            return c.cnt["dve"]

        def tap_fm(name, buf, deps):
            if name not in taps:
                return
            def _(g):
                for nm, v in deps:
                    c.w(g, nm, v)
                i = g.dma_start(out=dt_[name].rearrange("(ch p) t -> p ch t", p=128),
                                in_=buf[:].rearrange("p (ch t) -> p ch t", ch=NCH))
                c.inc("gst", i, 16)
            G_(_)

        def emit_l0_tern():
            ld1 = {}
            red = {}
            for ch in range(NCH):
                def gl(g, ch=ch):
                    if ch >= 2:
                        c.w(g, "dve", red[ch - 2])
                    i = g.dma_start(out=SCR[:, (ch % 2) * 1024:(ch % 2) * 1024 + 1024],
                                    in_=dt_["l0_ffnT"][ch * 128:(ch + 1) * 128, :])
                    ld1[ch] = (f"f{ch % 2}", c.inc(f"f{ch % 2}", i, 16))
                G_(gl)

                def vr(v, ch=ch):
                    c.w(v, ld1[ch][0], ld1[ch][1])
                    i = v.tensor_reduce(out=SM[:, 64 + ch:65 + ch],
                                        in_=SCR[:, (ch % 2) * 1024:(ch % 2) * 1024 + 1024],
                                        axis=AX, op=ALU.add, apply_absolute_value=True)
                    red[ch] = c.inc("dve", i)
                V_(vr)

            def vsum(v):
                c.w(v, "dve", red[NCH - 1])
                i = v.tensor_tensor(out=SM[:, 72:73], in0=SM[:, 64:65], in1=SM[:, 65:66], op=ALU.add)
                k = c.inc("dve", i)
                for j in range(2, NCH):
                    c.w(v, "dve", k)
                    i = v.tensor_tensor(out=SM[:, 72:73], in0=SM[:, 72:73], in1=SM[:, 64 + j:65 + j], op=ALU.add)
                    k = c.inc("dve", i)
            V_(vsum)
            RS = c.cnt["dve"]

            def td(t):
                c.w(t, "dve", RS)
                c.w(t, "pool", CONST_POOL)
                i = t.matmul(psB[0:1, 0:1], SM[:, 72:73], ONESfc[:], start=True, stop=True)
                c.inc("pe", i)
            T_(td)
            P1 = c.cnt["pe"]

            def sd(s):
                c.w(s, "pe", P1)
                i = s.activation(out=SM[0:1, 73:74], in_=psB[0:1, 0:1], func=AF.Identity, bias=0.0, scale=0.7 / (D * D))
                c.inc("act", i)
            S_(sd)
            A1 = c.cnt["act"]

            def td2(t):
                c.w(t, "act", A1)
                i = t.matmul(psB[:, 1:2], ONESfr[:], SM[0:1, 73:74], start=True, stop=True)
                c.inc("pe", i)
            T_(td2)
            P2 = c.cnt["pe"]

            def sd2(s):
                c.w(s, "pe", P2)
                i = s.activation(out=SM[:, 74:75], in_=psB[:, 1:2], func=AF.Identity, bias=0.0, scale=1.0)
                c.inc("act", i)
                i = s.activation(out=SM[:, 75:76], in_=psB[:, 1:2], func=AF.Identity, bias=0.0, scale=-1.0)
                c.inc("act", i)
            S_(sd2)
            A2 = c.cnt["act"]

            ld2 = {}
            sgn = {}
            msk = {}
            eff = {}
            for ch in range(NCH):
                def gl2(g, ch=ch):
                    if ch == 0:
                        c.w(g, "act", A2)
                        c.w(g, "dve", RS)
                    if ch >= 2:
                        c.w(g, "dve", eff[ch - 2])
                    i = g.dma_start(out=SCR[:, (ch % 2) * 1024:(ch % 2) * 1024 + 1024],
                                    in_=dt_["l0_ffnT"][ch * 128:(ch + 1) * 128, :])
                    ld2[ch] = (f"f{ch % 2}", c.inc(f"f{ch % 2}", i, 16))
                G_(gl2)

                def vt(v, ch=ch):
                    if ch == 0:
                        c.w(v, "act", A2)
                        c.w(v, "gio", CONST_GIO)
                    c.w(v, ld2[ch][0], ld2[ch][1])
                    wsrc = SCR[:, (ch % 2) * 1024:(ch % 2) * 1024 + 1024]
                    fA = Fb[:, (ch % 2) * 1024:(ch % 2) * 1024 + 1024]
                    fB = Fb[:, 2048 + (ch % 2) * 1024:2048 + (ch % 2) * 1024 + 1024]
                    if ch >= 2:
                        c.w(v, "dve", eff[ch - 2])
                    i = v.tensor_scalar(out=fA, in0=wsrc, scalar1=SM[:, 74:75], scalar2=None, op0=ALU.is_gt)
                    c.inc("dve", i)
                    i = v.tensor_scalar(out=fB, in0=wsrc, scalar1=SM[:, 75:76], scalar2=None, op0=ALU.is_lt)
                    k = c.inc("dve", i)
                    c.w(v, "dve", k)
                    i = v.tensor_tensor(out=fA, in0=fA, in1=fB, op=ALU.subtract)
                    k = c.inc("dve", i)
                    c.w(v, "dve", k)
                    i = v.tensor_scalar(out=WB[:, ch * 1024:(ch + 1) * 1024],
                                        in0=fA, scalar1=CB[:, 48:49], scalar2=None, op0=ALU.mult)
                    eff[ch] = c.inc("dve", i)
                V_(vt)
            TERN = c.cnt["dve"]
            return TERN

        def emit_l0_ffn_mm(xn_ready, TERN):
            mmf = {}
            evf = {}
            stf = {}
            act0_ffn = c.cnt["act"]
            for ofc in range(NCH):
                for tt in range(NT4):
                    evf[ofc * NT4 + tt] = act0_ffn + ofc * NT4 + tt + 1
            ffn_sq = []
            for ofc in range(NCH):
                for tt in range(NT4):
                    idx = ofc * NT4 + tt

                    def tf(t, ofc=ofc, tt=tt, idx=idx):
                        if idx == 0:
                            c.w(t, "dve", max(TERN, xn_ready))
                        if idx >= 2:
                            c.w(t, "act", evf[idx - 2])
                        bank = (idx % 2) * 512
                        for k in range(NCH):
                            i = t.matmul(psA[:, bank:bank + 512],
                                         WB[:, k * 1024 + ofc * 128:k * 1024 + (ofc + 1) * 128],
                                         Bb[:, k * TOK + tt * 512:k * TOK + (tt + 1) * 512],
                                         start=(k == 0), stop=(k == NCH - 1))
                        mmf[idx] = c.inc("pe", i)
                    T_(tf)

                    def sf(s, ofc=ofc, tt=tt, idx=idx):
                        c.w(s, "pe", mmf[idx])
                        if idx >= 4:
                            c.w(s, stf[idx - 4][0], stf[idx - 4][1])
                        i = s.activation(out=Fb[:, 4096 + (idx % 4) * 512:4096 + (idx % 4) * 512 + 512],
                                         in_=psA[:, (idx % 2) * 512:(idx % 2) * 512 + 512], func=AF.Silu,
                                         bias=CB[:, 49 + ofc:50 + ofc], scale=1.0)
                        got = c.inc("act", i)
                        assert got == evf[idx]
                    ffn_sq.append(sf)

                    def yf(sy, ofc=ofc, tt=tt, idx=idx):
                        c.w(sy, "act", evf[idx])
                        i = sy.dma_start(out=dt_["g_act"][ofc * 128:(ofc + 1) * 128, tt * 512:(tt + 1) * 512],
                                         in_=Fb[:, 4096 + (idx % 4) * 512:4096 + (idx % 4) * 512 + 512])
                        stf[idx] = (f"gs{idx % 4}", c.inc(f"gs{idx % 4}", i, 16))
                    SY_(yf)

            def sflush(s):
                for f in ffn_sq:
                    f(s)
            S_(sflush)
            return c.cnt["gs3"], c.cnt["pe"], c.cnt["act"]

        def emit_qkv(layer, xn_ready, prev_users_done):
            wname = "l0_wqkvT" if layer == 0 else "l1_wqkvT"
            wld = {}
            slot_last_use = {0: 0, 1: 0}
            qk_mm = {}
            qk_sq = {}
            qk_cp = {}
            rms_mm = {}
            rms_ln = {}
            sel_mm = {}
            stt_d = {}
            v_mm = {}
            v_ev = {}
            act0_q = c.cnt["act"]
            if layer == 0:
                for ofc_ in range(16):
                    qk_sq[ofc_] = act0_q + ofc_ * 4 + 1
                    qk_cp[ofc_] = act0_q + ofc_ * 4 + 2
                    rms_ln[ofc_] = act0_q + ofc_ * 4 + 4
                for vidx in range(32):
                    v_ev[vidx] = act0_q + 64 + vidx + 1
            else:
                for idx in range(64):
                    qk_cp[idx] = act0_q + idx + 1
                for vidx in range(32):
                    v_ev[vidx] = act0_q + 64 + vidx + 1
            qkv_sq = []
            gs_fin = {nm: c.cnt[nm] for nm in ("gs0", "gs1", "gs2", "gs3")} if layer == 0 else {}

            def vo(v):
                for nm, val in prev_users_done:
                    c.w(v, nm, val)
                for pt in range(NPT):
                    i = v.memset(Dv[:, pt * 1040:(pt + 1) * 1040]
                                 .rearrange("p (h e) -> p h e", e=65)[:, :, 64:65], 1.0)
                    c.inc("dve", i)
            V_(vo)
            VONES = c.cnt["dve"]

            for sl in range(6):
                def gw(g, sl=sl):
                    if sl == 0:
                        for nm, val in prev_users_done:
                            c.w(g, nm, val)
                    if sl >= 2:
                        c.w(g, "pe", slot_last_use[sl % 2])
                    i = g.dma_start(out=WB[:, (sl % 2) * 4096:(sl % 2) * 4096 + 4096]
                                    .rearrange("p (k o) -> p k o", k=8),
                                    in_=dt_[wname][:, sl * 512:(sl + 1) * 512]
                                    .rearrange("(k p) o -> p k o", p=128))
                    wld[sl] = (f"qw{sl % 2}", c.inc(f"qw{sl % 2}", i, 16))
                G_(gw)

                if sl < 4:
                    if layer == 0:
                        def mk_tq(ofc, sl_):
                            pp = ofc % 2

                            def tq(t):
                                if ofc == 0:
                                    c.w(t, "dve", max(xn_ready, VONES))
                                    for nm_, v_ in prev_users_done:
                                        c.w(t, nm_, v_)
                                c.w(t, wld[sl_][0], wld[sl_][1])
                                if ofc > 1:
                                    c.w(t, "act", qk_cp[ofc - 2])
                                    c.w(t, "dve", stt_d[ofc - 2])
                                ps_ = psA if pp == 0 else psB
                                for tt in range(NT4):
                                    for k in range(NCH):
                                        i = t.matmul(ps_[:, tt * 512:(tt + 1) * 512],
                                                     WB[:, (sl_ % 2) * 4096 + k * 512 + (ofc % 4) * 128:
                                                        (sl_ % 2) * 4096 + k * 512 + (ofc % 4 + 1) * 128],
                                                     Bb[:, k * TOK + tt * 512:k * TOK + (tt + 1) * 512],
                                                     start=(k == 0), stop=(k == NCH - 1))
                                qk_mm[ofc] = c.inc("pe", i)
                                slot_last_use[sl_ % 2] = qk_mm[ofc]
                            return tq

                        def mk_tail(ofc):
                            pp = ofc % 2

                            def sq0(s):
                                ps_ = psA if pp == 0 else psB
                                c.w(s, "pe", qk_mm[ofc])
                                if ofc == 0:
                                    for nm_, v_ in gs_fin.items():
                                        c.w(s, nm_, v_)
                                if ofc >= 2:
                                    c.w(s, "pe", rms_mm.get(ofc - 2, 0))
                                bcol = CB[:, ofc:ofc + 1]
                                i = s.activation(out=Fb[:, 4096 + pp * 2048:4096 + pp * 2048 + 2048],
                                                 in_=ps_[:], func=AF.Square, bias=bcol, scale=1.0)
                                assert c.inc("act", i) == qk_sq[ofc]
                                if ofc >= 2:
                                    c.w(s, "dve", stt_d[ofc - 2])
                                i = s.activation(out=Fb[:, pp * 2048:pp * 2048 + 2048],
                                                 in_=ps_[:], func=AF.Identity, bias=bcol, scale=1.0)
                                assert c.inc("act", i) == qk_cp[ofc]
                            qkv_sq.append(sq0)

                            def tr0(t):
                                psR = psA if pp == 0 else psB
                                c.w(t, "act", qk_cp[ofc])
                                for tt in range(NT4):
                                    i = t.matmul(psR[0:2, tt * 512:(tt + 1) * 512], BLK2B[:],
                                                 Fb[:, 4096 + pp * 2048 + tt * 512:4096 + pp * 2048 + (tt + 1) * 512],
                                                 start=True, stop=True)
                                rms_mm[ofc] = c.inc("pe", i)
                            T_(tr0)

                            def sr0(s):
                                psR = psA if pp == 0 else psB
                                c.w(s, "pe", rms_mm[ofc])
                                if ofc >= 2:
                                    c.w(s, "pe", sel_mm.get(ofc - 2, 0))
                                i = s.activation(out=SCR[64:66, pp * 2048:pp * 2048 + 2048],
                                                 in_=psR[0:2, :],
                                                 func=AF.Ln, bias=CB[0:2, 68:69], scale=1.0 / HD)
                                k = c.inc("act", i)
                                c.w(s, "act", k)
                                i = s.activation(out=SCR[0:2, pp * 2048:pp * 2048 + 2048],
                                                 in_=SCR[64:66, pp * 2048:pp * 2048 + 2048],
                                                 func=AF.Exp, bias=0.0, scale=-0.5)
                                assert c.inc("act", i) == rms_ln[ofc]
                            qkv_sq.append(sr0)

                            def ts0(t):
                                psR = psA if pp == 0 else psB
                                c.w(t, "act", rms_ln[ofc])
                                for tt in range(NT4):
                                    i = t.matmul(psR[:, tt * 512:(tt + 1) * 512], SEL[:],
                                                 SCR[0:2, pp * 2048 + tt * 512:pp * 2048 + (tt + 1) * 512],
                                                 start=True, stop=True)
                                sel_mm[ofc] = c.inc("pe", i)
                            T_(ts0)

                            def vs0(v):
                                psR = psA if pp == 0 else psB
                                c.w(v, "pe", sel_mm[ofc])
                                c.w(v, "act", qk_cp[ofc])
                                dst = A if ofc < 8 else C
                                dch = ofc if ofc < 8 else ofc - 8
                                pat = CB[:, 32 + ofc:33 + ofc] if ofc < 8 else CB[:, 40 + ofc - 8:41 + ofc - 8]
                                i = v.scalar_tensor_tensor(
                                    out=dst[:, dch * TOK:(dch + 1) * TOK],
                                    in0=Fb[:, pp * 2048:pp * 2048 + 2048],
                                    scalar=pat, in1=psR[:],
                                    op0=ALU.mult, op1=ALU.mult)
                                stt_d[ofc] = c.inc("dve", i)
                            V_(vs0)

                        for msub in range(4):
                            ofc = sl * 4 + msub
                            T_(mk_tq(ofc, sl))
                            if ofc >= 1:
                                mk_tail(ofc - 1)
                            if ofc == 15:
                                mk_tail(15)
                    else:
                        for msub in range(4):
                            ofc = sl * 4 + msub
                            for tt in range(NT4):
                                idx = ofc * NT4 + tt

                                def tq(t, sl=sl, msub=msub, ofc=ofc, tt=tt, idx=idx):
                                    if idx == 0:
                                        c.w(t, "dve", max(xn_ready, VONES))
                                        for nm_, v_ in prev_users_done:
                                            c.w(t, nm_, v_)
                                    c.w(t, wld[sl][0], wld[sl][1])
                                    if idx > 1:
                                        c.w(t, "act", qk_cp[idx - 2])
                                    for k in range(NCH):
                                        i = t.matmul(psA[:, (idx % 2) * 512:(idx % 2) * 512 + 512],
                                                     WB[:, (sl % 2) * 4096 + k * 512 + msub * 128:
                                                        (sl % 2) * 4096 + k * 512 + (msub + 1) * 128],
                                                     Bb[:, k * TOK + tt * 512:k * TOK + (tt + 1) * 512],
                                                     start=(k == 0), stop=(k == NCH - 1))
                                    qk_mm[idx] = c.inc("pe", i)
                                    slot_last_use[sl % 2] = qk_mm[idx]
                                T_(tq)

                                def sq1(s, ofc=ofc, tt=tt, idx=idx):
                                    c.w(s, "pe", qk_mm[idx])
                                    dst = A if ofc < 8 else C
                                    dch = ofc if ofc < 8 else ofc - 8
                                    i = s.activation(out=dst[:, dch * TOK + tt * 512:dch * TOK + (tt + 1) * 512],
                                                     in_=psA[:, (idx % 2) * 512:(idx % 2) * 512 + 512],
                                                     func=AF.Identity, bias=0.0, scale=1.0)
                                    assert c.inc("act", i) == qk_cp[idx]
                                qkv_sq.append(sq1)
                else:
                    for pt in range(NPT):
                        vidx = (sl - 4) * NPT + pt

                        def tv(t, sl=sl, pt=pt, vidx=vidx):
                            c.w(t, wld[sl][0], wld[sl][1])
                            if vidx == 0:
                                for nm_, v_ in prev_users_done:
                                    c.w(t, nm_, v_)
                                if layer == 0:
                                    c.w(t, "dve", stt_d.get(14, 0))
                            if vidx > 1:
                                c.w(t, "act", v_ev[vidx - 2])
                            for k in range(NCH):
                                i = t.matmul(psA[:, 1024 + (vidx % 2) * 512:1024 + (vidx % 2) * 512 + 512],
                                             Bb[:, k * TOK + pt * 128:k * TOK + (pt + 1) * 128],
                                             WB[:, (sl % 2) * 4096 + k * 512:(sl % 2) * 4096 + (k + 1) * 512],
                                             start=(k == 0), stop=(k == NCH - 1 and layer == 1))
                            if layer == 0:
                                c.w(t, "pool", CONST_POOL)
                                i = t.matmul(psA[:, 1024 + (vidx % 2) * 512:1024 + (vidx % 2) * 512 + 512],
                                             ONESbr[:],
                                             BROW[0:1, (sl - 4) * 512:(sl - 4) * 512 + 512],
                                             start=False, stop=True)
                            v_mm[vidx] = c.inc("pe", i)
                            slot_last_use[sl % 2] = v_mm[vidx]
                        T_(tv)

                        def sv(s, sl=sl, pt=pt, vidx=vidx):
                            c.w(s, "pe", v_mm[vidx])
                            h0 = (sl - 4) * 8
                            dstap = Dv[:, pt * 1040 + h0 * 65:pt * 1040 + h0 * 65 + 8 * 65] \
                                .rearrange("p (h e) -> p h e", e=65)[:, :, 0:64]
                            i = s.activation(out=dstap, in_=psA[:, 1024 + (vidx % 2) * 512:1024 + (vidx % 2) * 512 + 512],
                                             func=AF.Identity, bias=0.0, scale=1.0)
                            assert c.inc("act", i) == v_ev[vidx]
                        qkv_sq.append(sv)

            def qflush(s):
                for f in qkv_sq:
                    f(s)
            S_(qflush)
            return c.cnt["dve"], c.cnt["act"]

        def emit_attn(layer, deps):
            scale = L0_SCALE if layer == 0 else L1_SCALE
            act0 = c.cnt["act"]
            exp_done = {}
            rawdve = {}
            recz = {}
            for pr in range(8):
                for sqh in range(2):
                    base = act0 + pr * 68 + sqh * 34
                    for skt in range(NPT):
                        for eo in range(2):
                            exp_done[(pr, sqh, skt, eo)] = base + skt * 2 + eo + 1
                    recz[(pr, sqh, 1)] = base + 34
            sc_mm = {}
            av_mm = {}
            bcd = {}
            att_d = {}
            scalar_q = []
            pending = []

            for pr in range(8):
                for sqh in range(2):
                    for skt in range(NPT):
                        def tsc(t, pr=pr, sqh=sqh, skt=skt):
                            if pr == 0 and sqh == 0 and skt == 0:
                                for nm, val in deps:
                                    c.w(t, nm, val)
                            def _prev(eo):
                                if skt > 0:
                                    return exp_done[(pr, sqh, skt - 1, eo)]
                                if sqh == 1:
                                    return exp_done[(pr, 0, NPT - 1, eo)]
                                if pr > 0:
                                    return exp_done[(pr - 1, 1, NPT - 1, eo)]
                                return 0
                            c.w(t, "act", _prev(0))
                            for sqt in range(2):
                                sq0 = sqh * 1024 + sqt * 512
                                i = t.matmul(psA[:, sqt * 512:(sqt + 1) * 512],
                                             C[0:64, pr * TOK + skt * 128:pr * TOK + (skt + 1) * 128],
                                             A[0:64, pr * TOK + sq0:pr * TOK + sq0 + 512],
                                             start=True, stop=True, tile_position=(0, 0))
                            sc_mm[(pr, sqh, skt, 0)] = c.inc("pe", i)
                            c.w(t, "act", _prev(1))
                            for sqt in range(2):
                                sq0 = sqh * 1024 + sqt * 512
                                i = t.matmul(psA[:, 1024 + sqt * 512:1024 + (sqt + 1) * 512],
                                             C[64:128, pr * TOK + skt * 128:pr * TOK + (skt + 1) * 128],
                                             A[64:128, pr * TOK + sq0:pr * TOK + sq0 + 512],
                                             start=True, stop=True, tile_position=(64, 0))
                            sc_mm[(pr, sqh, skt, 1)] = c.inc("pe", i)
                            if skt > 0:
                                ps_ = skt - 1
                                w = exp_done[(pr, sqh, ps_, 1)]
                                if ps_ == 0:
                                    if sqh == 1:
                                        w = max(w, recz[(pr, 0, 1)])
                                    elif pr > 0:
                                        w = max(w, recz[(pr - 1, 1, 1)])
                                c.w(t, "act", w)
                                if ps_ == 0:
                                    if sqh == 1 and (pr, 0) in att_d:
                                        c.w(t, "dve", att_d[(pr, 0)])
                                    elif sqh == 0 and pr > 0 and (pr - 1, 1) in att_d:
                                        c.w(t, "dve", att_d[(pr - 1, 1)])
                                for eo in range(2):
                                    h = pr * 2 + eo
                                    for sqt in range(2):
                                        i = t.matmul(psB[0:65, eo * 1024 + sqt * 512:eo * 1024 + (sqt + 1) * 512],
                                                     Dv[:, ps_ * 1040 + h * 65:ps_ * 1040 + h * 65 + 65],
                                                     Eb[:, (ps_ % 2) * 2048 + eo * 1024 + sqt * 512:
                                                        (ps_ % 2) * 2048 + eo * 1024 + (sqt + 1) * 512],
                                                     start=(ps_ == 0), stop=False)
                                av_mm[(pr, sqh, ps_)] = c.inc("pe", i)
                        T_(tsc)
                        if skt == 0 and pending:
                            T_(pending[0])
                            V_(pending[1])
                            pending = []

                        def sex(s, pr=pr, sqh=sqh, skt=skt):
                            for eo in range(2):
                                w = sc_mm[(pr, sqh, skt, eo)]
                                if eo == 0 and skt >= 2:
                                    w = max(w, av_mm[(pr, sqh, skt - 2)])
                                c.w(s, "pe", w)
                                i = s.activation(out=Eb[:, (skt % 2) * 2048 + eo * 1024:(skt % 2) * 2048 + eo * 1024 + 1024],
                                                 in_=psA[:, eo * 1024:eo * 1024 + 1024],
                                                 func=AF.Exp, bias=0.0, scale=scale)
                                got = c.inc("act", i)
                                assert got == exp_done[(pr, sqh, skt, eo)]
                        scalar_q.append(sex)

                    def tav(t, pr=pr, sqh=sqh):
                        ps_ = NPT - 1
                        c.w(t, "act", exp_done[(pr, sqh, ps_, 1)])
                        for eo in range(2):
                            h = pr * 2 + eo
                            for sqt in range(2):
                                i = t.matmul(psB[0:65, eo * 1024 + sqt * 512:eo * 1024 + (sqt + 1) * 512],
                                             Dv[:, ps_ * 1040 + h * 65:ps_ * 1040 + h * 65 + 65],
                                             Eb[:, (ps_ % 2) * 2048 + eo * 1024 + sqt * 512:
                                                (ps_ % 2) * 2048 + eo * 1024 + (sqt + 1) * 512],
                                             start=False, stop=True)
                        av_mm[(pr, sqh, ps_)] = c.inc("pe", i)
                    T_(tav)

                    def vraw(v, pr=pr, sqh=sqh):
                        c.w(v, "pe", av_mm[(pr, sqh, NPT - 1)])
                        if (pr, sqh) != (0, 0):
                            pk = (pr, 0) if sqh == 1 else (pr - 1, 1)
                            c.w(v, "dve", att_d[pk])
                        i = v.tensor_copy(out=SCR[0:64, 0:2048], in_=psB[0:64, :])
                        rawdve[(pr, sqh)] = c.inc("dve", i)
                    V_(vraw)

                    def sraw(s, pr=pr, sqh=sqh):
                        c.w(s, "pe", av_mm[(pr, sqh, NPT - 1)])
                        i = s.activation(out=SCR[32:33, 2048:4096], in_=psB[64:65, 0:2048],
                                         func=AF.Ln, bias=CB[0:1, 69:70], scale=1.0)
                        k = c.inc("act", i)
                        c.w(s, "act", k)
                        i = s.activation(out=SCR[0:1, 2048:4096], in_=SCR[32:33, 2048:4096],
                                         func=AF.Exp, bias=0.0, scale=-1.0)
                        got = c.inc("act", i)
                        assert got == recz[(pr, sqh, 1)]
                    scalar_q.append(sraw)

                    def tbc(t, pr=pr, sqh=sqh):
                        c.w(t, "act", recz[(pr, sqh, 1)])
                        c.w(t, "dve", rawdve[(pr, sqh)])
                        for eo in range(2):
                            for sqt in range(2):
                                i = t.matmul(psB[0:64, eo * 1024 + sqt * 512:eo * 1024 + (sqt + 1) * 512],
                                             ONESfr[0:1, 0:64],
                                             SCR[0:1, 2048 + eo * 1024 + sqt * 512:2048 + eo * 1024 + (sqt + 1) * 512],
                                             start=True, stop=True)
                        bcd[(pr, sqh)] = c.inc("pe", i)

                    def vn(v, pr=pr, sqh=sqh):
                        c.w(v, "pe", bcd[(pr, sqh)])
                        for eo in range(2):
                            i = v.tensor_tensor(out=A[64 * eo:64 * eo + 64,
                                                      pr * TOK + sqh * 1024:pr * TOK + sqh * 1024 + 1024],
                                                in0=SCR[0:64, eo * 1024:eo * 1024 + 1024],
                                                in1=psB[0:64, eo * 1024:eo * 1024 + 1024], op=ALU.mult)
                            att_d[(pr, sqh)] = c.inc("dve", i)
                    pending = [tbc, vn]
            T_(pending[0])
            V_(pending[1])

            def sall(s):
                for f in scalar_q:
                    f(s)
            S_(sall)
            return c.cnt["dve"]

        def emit_proj_residual(layer, attn_ready, extra_deps=()):
            wname = "l0_wprojT" if layer == 0 else "l1_woT"
            x_src = dt_["xT"] if layer == 0 else dt_["x_mid"]
            out_d = dt_["x_mid"] if layer == 0 else dt_["outT"]
            out_f32 = (layer == 1)
            wld = {}
            gld = {}
            xld = {}
            mmp = {}
            sgd = {}
            res = {}
            std = {}

            def gw(g):
                c.w(g, "dve", attn_ready)
                for nm_, v_ in extra_deps:
                    c.w(g, nm_, v_)
                for sl in range(2):
                    i = g.dma_start(out=WB[:, sl * 4096:(sl + 1) * 4096].rearrange("p (k o) -> p k o", k=8),
                                    in_=dt_[wname][:, sl * 512:(sl + 1) * 512].rearrange("(k p) o -> p k o", p=128))
                    wld[sl] = (f"qw{sl % 2}", c.inc(f"qw{sl % 2}", i, 16))
            G_(gw)

            for ofc in range(NCH):
                def gx(g, ofc=ofc):
                    if ofc < 2:
                        for nm_ in ("gs0", "gs1", "gs2", "gs3", "g10", "g11", "g12", "g13"):
                            if nm_ in c.cnt:
                                c.w(g, nm_)
                    if ofc >= 2:
                        c.w(g, "dve", max(sgd[ofc - 2], res[ofc - 2]))
                    i = g.dma_start(out=Fb[:, (ofc % 2) * 2048:(ofc % 2) * 2048 + 2048],
                                    in_=dt_["g_act"][ofc * 128:(ofc + 1) * 128, :])
                    c.inc(f"pg{ofc % 2}", i, 16)
                    i = g.dma_start(out=Fb[:, 4096 + (ofc % 2) * 2048:4096 + (ofc % 2) * 2048 + 2048],
                                    in_=x_src[ofc * 128:(ofc + 1) * 128, :])
                    xld[ofc] = (f"pg{ofc % 2}", c.inc(f"pg{ofc % 2}", i, 16))
                G_(gx)

                def tp(t, ofc=ofc):
                    sl, msub = ofc // 4, ofc % 4
                    c.w(t, wld[sl][0], wld[sl][1])
                    if ofc >= 2:
                        c.w(t, "dve", sgd[ofc - 2])
                    ps = psA if ofc % 2 == 0 else psB
                    for tt in range(NT4):
                        for k in range(NCH):
                            i = t.matmul(ps[:, tt * 512:(tt + 1) * 512],
                                         WB[:, sl * 4096 + k * 512 + msub * 128:sl * 4096 + k * 512 + (msub + 1) * 128],
                                         A[:, k * TOK + tt * 512:k * TOK + (tt + 1) * 512],
                                         start=(k == 0), stop=(k == NCH - 1))
                    mmp[ofc] = c.inc("pe", i)
                T_(tp)

                def vp(v, ofc=ofc):
                    ps = psA if ofc % 2 == 0 else psB
                    c.w(v, "pe", mmp[ofc])
                    c.w(v, xld[ofc][0], xld[ofc][1])
                    bias = CB[:, 24 + ofc:25 + ofc] if layer == 0 else 0.0
                    i = v.scalar_tensor_tensor(out=Eb[:, (ofc % 2) * 2048:(ofc % 2) * 2048 + 2048],
                                               in0=ps[:], scalar=bias,
                                               in1=Fb[:, (ofc % 2) * 2048:(ofc % 2) * 2048 + 2048],
                                               op0=ALU.add, op1=ALU.mult)
                    sgd[ofc] = c.inc("dve", i)
                    if ofc >= 2:
                        c.w(v, std[ofc - 2][0], std[ofc - 2][1])
                    c.w(v, "dve", sgd[ofc])
                    i = v.tensor_tensor(out=SCR[:, (ofc % 2) * 2048:(ofc % 2) * 2048 + 2048],
                                        in0=Fb[:, 4096 + (ofc % 2) * 2048:4096 + (ofc % 2) * 2048 + 2048],
                                        in1=Eb[:, (ofc % 2) * 2048:(ofc % 2) * 2048 + 2048], op=ALU.add)
                    res[ofc] = c.inc("dve", i)
                V_(vp)

                def gs(g, ofc=ofc):
                    c.w(g, "dve", res[ofc])
                    i = g.dma_start(out=out_d[ofc * 128:(ofc + 1) * 128, :],
                                    in_=SCR[:, (ofc % 2) * 2048:(ofc % 2) * 2048 + 2048])
                    std[ofc] = (f"xs{ofc % 2}", c.inc(f"xs{ofc % 2}", i, 16))
                    c.inc("gst", i, 0) if False else None
                G_(gs)
            return c.cnt["gst"]

        def emit_qpk(xn_ready, prev_done):
            def gw(g):
                for nm, v in prev_done:
                    c.w(g, nm, v)
                i = g.dma_start(out=WB[:, 0:4096].rearrange("p (k o) -> p k o", k=8),
                                in_=dt_["wq_p"].rearrange("(k p) o -> p k o", p=128))
                c.inc("gio", i, 16)
            G_(gw)
            WLD = c.cnt["gio"]
            mmq = {}
            evq = {}
            act0_p = c.cnt["act"]
            for i_ in range(16):
                evq[i_] = act0_p + i_ + 1
            qpk_sq = []
            for hch in range(4):
                for tt in range(NT4):
                    idx = hch * NT4 + tt

                    def tq(t, hch=hch, tt=tt, idx=idx):
                        if idx == 0:
                            c.w(t, "gio", WLD)
                            c.w(t, "dve", xn_ready)
                        if idx >= 2:
                            c.w(t, "act", evq[idx - 2])
                        bank = (idx % 2) * 512
                        for k in range(NCH):
                            i = t.matmul(psA[:, bank:bank + 512],
                                         WB[:, k * 512 + hch * 128:k * 512 + (hch + 1) * 128],
                                         Bb[:, k * TOK + tt * 512:k * TOK + (tt + 1) * 512],
                                         start=(k == 0), stop=(k == NCH - 1))
                        mmq[idx] = c.inc("pe", i)
                    T_(tq)

                    def sq(s, hch=hch, tt=tt, idx=idx):
                        c.w(s, "pe", mmq[idx])
                        i = s.activation(out=Eb[:, hch * TOK + tt * 512:hch * TOK + (tt + 1) * 512],
                                         in_=psA[:, (idx % 2) * 512:(idx % 2) * 512 + 512],
                                         func=AF.Identity, bias=0.0, scale=1.0)
                        assert c.inc("act", i) == evq[idx]
                    qpk_sq.append(sq)

            def pflush(s):
                for f in qpk_sq:
                    f(s)
            S_(pflush)
            return c.cnt["act"]

        def emit_lists(qpk_ready):
            simmm = {}
            simev = {}
            lists_dve = {}
            act0_l = c.cnt["act"]
            E8F = {pt_: act0_l + pt_ * 5 + 4 for pt_ in range(NPT)}
            LNZF = {pt_: act0_l + pt_ * 5 + 5 for pt_ in range(NPT)}
            lists_sq = []
            for pt in range(NPT):
                V8 = lambda pt, h: SCR[:, 1024 + pt * 40 + h * 8:1024 + pt * 40 + h * 8 + 8]
                NEGMC = lambda pt, h: SCR[:, 1664 + pt * 4 + h:1664 + pt * 4 + h + 1]
                E8C = lambda pt, h: SM[:, 756 + pt * 32 + h * 8:756 + pt * 32 + h * 8 + 8]
                ZC = lambda pt: SCR[:, 1856 + pt * 4:1856 + pt * 4 + 4]
                LNZC = lambda pt: SCR[:, 1728 + pt * 4:1728 + pt * 4 + 4]

                def tsim(t, pt=pt):
                    if pt == 0:
                        c.w(t, "act", qpk_ready)
                    if pt > 0:
                        c.w(t, "dve", simev[pt - 1])
                    for h in range(4):
                        for p in range(2):
                            i = t.matmul(psB[:, p * 512 + h * 64:p * 512 + (h + 1) * 64],
                                         Eb[64 * p:64 * p + 64, h * TOK + pt * 128:h * TOK + (pt + 1) * 128],
                                         KEYSP[64 * p:64 * p + 64, h * 64:(h + 1) * 64],
                                         start=True, stop=True, tile_position=(64 * p, 0))
                    simmm[pt] = c.inc("pe", i)
                T_(tsim)

                def vtop(v, pt=pt, V8=V8, NEGMC=NEGMC):
                    c.w(v, "pe", simmm[pt])
                    if pt > 0:
                        c.w(v, "dve", lists_dve[pt - 1])
                    i = v.tensor_copy(out=SCR[:, 0:1024], in_=psB[:, 0:1024])
                    simev[pt] = c.inc("dve", i)
                    c.w(v, "dve", simev[pt])
                    for ph in range(8):
                        h_, p_ = ph // 2, ph % 2
                        i = v.max(out=SM[:, 128 + ph * 8:128 + ph * 8 + 8],
                                  in_=SCR[:, p_ * 512 + h_ * 64:p_ * 512 + (h_ + 1) * 64])
                        c.inc("dve", i)
                    k = c.cnt["dve"]
                    for h in range(4):
                        c.w(v, "dve", k)
                        i = v.tensor_tensor(out=SM[:, 192 + h * 64:192 + h * 64 + 64]
                                            .rearrange("p (a b) -> p a b", a=8),
                                            in0=SM[:, 128 + 2 * h * 8:128 + 2 * h * 8 + 8]
                                            .unsqueeze(2).to_broadcast([128, 8, 8]),
                                            in1=SM[:, 128 + (2 * h + 1) * 8:128 + (2 * h + 1) * 8 + 8]
                                            .unsqueeze(1).to_broadcast([128, 8, 8]),
                                            op=ALU.add)
                        c.inc("dve", i)
                    k = c.cnt["dve"]
                    for h in range(4):
                        c.w(v, "dve", k)
                        i = v.max(out=V8(pt, h), in_=SM[:, 192 + h * 64:192 + h * 64 + 64])
                        k2 = c.inc("dve", i)
                        c.w(v, "dve", k2)
                        i = v.match_replace(out=SM[:, 480:544], in_to_replace=V8(pt, h),
                                            in_values=SM[:, 192 + h * 64:192 + h * 64 + 64], imm_value=-1e30)
                        k2 = c.inc("dve", i)
                        c.w(v, "dve", k2)
                        i = v.reduce_max(out=SM[:, 544 + h:545 + h], in_=SM[:, 480:544], axis=AX)
                        k2 = c.inc("dve", i)
                        c.w(v, "dve", k2)
                        i = v.match_replace(out=SM[:, 480:544], in_to_replace=SM[:, 128 + 2 * h * 8:128 + 2 * h * 8 + 8],
                                            in_values=SCR[:, h * 64:(h + 1) * 64], imm_value=-1e30)
                        k2 = c.inc("dve", i)
                        c.w(v, "dve", k2)
                        i = v.reduce_max(out=SM[:, 736 + h:737 + h], in_=SM[:, 480:544], axis=AX)
                        k2 = c.inc("dve", i)
                        c.w(v, "dve", k2)
                        i = v.match_replace(out=SM[:, 480:544], in_to_replace=SM[:, 128 + (2 * h + 1) * 8:128 + (2 * h + 1) * 8 + 8],
                                            in_values=SCR[:, 512 + h * 64:512 + (h + 1) * 64], imm_value=-1e30)
                        k2 = c.inc("dve", i)
                        c.w(v, "dve", k2)
                        i = v.reduce_max(out=SM[:, 740 + h:741 + h], in_=SM[:, 480:544], axis=AX)
                        k2 = c.inc("dve", i)
                        c.w(v, "dve", k2)
                    xy8 = SM[:, 128:192].rearrange("p (ph e) -> p ph e", e=8)
                    y1v = xy8[:, 1::2, 0:1]
                    x1v = xy8[:, 0::2, 0:1]
                    v8blk = SCR[:, 1024 + pt * 40:1024 + pt * 40 + 32].rearrange("p (h e) -> p h e", e=8)
                    c.w(v, "dve", c.cnt["dve"])
                    i = v.tensor_tensor(out=SM[:, 744:748].unsqueeze(2), in0=SM[:, 736:740].unsqueeze(2),
                                        in1=y1v, op=ALU.add)
                    c.inc("dve", i)
                    i = v.tensor_tensor(out=SM[:, 748:752].unsqueeze(2), in0=SM[:, 740:744].unsqueeze(2),
                                        in1=x1v, op=ALU.add)
                    k2 = c.inc("dve", i)
                    c.w(v, "dve", k2)
                    i = v.tensor_tensor(out=SM[:, 752:756], in0=SM[:, 744:748],
                                        in1=SM[:, 748:752], op=ALU.max)
                    k2 = c.inc("dve", i)
                    c.w(v, "dve", k2)
                    i = v.tensor_tensor(out=SM[:, 544:548], in0=SM[:, 544:548],
                                        in1=SM[:, 752:756], op=ALU.max)
                    k2 = c.inc("dve", i)
                    c.w(v, "dve", k2)
                    i = v.tensor_tensor(out=SM[:, 548:552].unsqueeze(2),
                                        in0=v8blk[:, :, 7:8],
                                        in1=SM[:, 544:548].unsqueeze(2), op=ALU.add)
                    k2 = c.inc("dve", i)
                    c.w(v, "dve", k2)
                    i = v.tensor_scalar(out=SM[:, 560 + pt * 4:564 + pt * 4],
                                        in0=SM[:, 548:552], scalar1=0.5, scalar2=None, op0=ALU.mult)
                    c.inc("dve", i)
                    i = v.tensor_scalar(out=SCR[:, 1664 + pt * 4:1668 + pt * 4].unsqueeze(2),
                                        in0=v8blk[:, :, 0:1],
                                        scalar1=-1.0, scalar2=None, op0=ALU.mult)
                    c.inc("dve", i)
                    lists_dve[pt] = c.cnt["dve"]
                V_(vtop)
                NEGM = c.cnt["dve"]

                def sexp(s, pt=pt, NEGM=NEGM, V8=V8, NEGMC=NEGMC, E8C=E8C, ZC=ZC, LNZC=LNZC):
                    c.w(s, "dve", NEGM)
                    for h in range(4):
                        i = s.activation(out=E8C(pt, h), in_=V8(pt, h),
                                         func=AF.Exp, bias=NEGMC(pt, h), scale=1.0)
                        c.inc("act", i)
                lists_sq.append(sexp)

                def vsum(v, pt=pt, E8C=E8C, ZC=ZC):
                    c.w(v, "act", E8F[pt])
                    for h in range(4):
                        i = v.reduce_sum(out=ZC(pt)[:, h:h + 1], in_=E8C(pt, h), axis=AX)
                        c.inc("dve", i)
                V_(vsum)
                ZS = c.cnt["dve"]

                def sln(s, pt=pt, ZS=ZS, ZC=ZC, LNZC=LNZC):
                    c.w(s, "dve", ZS)
                    i = s.activation(out=LNZC(pt), in_=ZC(pt), func=AF.Ln, bias=0.0, scale=1.0)
                    assert c.inc("act", i) == LNZF[pt]
                lists_sq.append(sln)

                def vnegc(v, pt=pt, NEGMC=NEGMC, LNZC=LNZC):
                    c.w(v, "act", LNZF[pt])
                    i = v.tensor_tensor(out=SM[:, 624 + pt * 4:628 + pt * 4],
                                        in0=SCR[:, 1664 + pt * 4:1668 + pt * 4],
                                        in1=SCR[:, 1728 + pt * 4:1732 + pt * 4],
                                        op=ALU.subtract)
                    c.inc("dve", i)
                    lists_dve[pt] = c.cnt["dve"]
                V_(vnegc)

            def lflush(s):
                for f in lists_sq:
                    f(s)
            S_(lflush)
            return c.cnt["dve"]

        def emit_wfull(xn_ready):
            def gk(g):
                c.w(g, "dve", xn_ready)
                i = g.dma_start(out=C[:].rearrange("p (h e) -> p h e", h=4),
                                in_=dt_["ktp"].rearrange("(h p) e -> p h e", p=128))
                c.inc("gio", i, 16)
            G_(gk)
            KTP_LD = c.cnt["gio"]
            slot_use = {0: 0, 1: 0}
            wld = {}
            mmw = {}
            evw = {}
            stw = {}
            act0_w = c.cnt["act"]
            for i_ in range(NCH * NPT):
                evw[i_] = act0_w + i_ + 1
            wf_sq = []
            for et in range(NCH):
                def gw(g, et=et):
                    if et >= 2:
                        c.w(g, "pe", slot_use[et % 2])
                    i = g.dma_start(out=WB[:, (et % 2) * 4096:(et % 2) * 4096 + 4096]
                                    .rearrange("p (k o) -> p k o", k=8),
                                    in_=dt_["down_T"][:, et * 512:(et + 1) * 512]
                                    .rearrange("(k p) o -> p k o", p=128))
                    wld[et] = (f"dw{et % 2}", c.inc(f"dw{et % 2}", i, 16))
                G_(gw)
                for pt in range(NPT):
                    idx = et * NPT + pt

                    def tw(t, et=et, pt=pt, idx=idx):
                        if idx == 0:
                            c.w(t, "dve", xn_ready)
                            c.w(t, "act", c.cnt["act"])
                        c.w(t, wld[et][0], wld[et][1])
                        if idx >= 2:
                            c.w(t, "act", evw[idx - 2])
                        bank = (idx % 2) * 512
                        for k in range(NCH):
                            i = t.matmul(psA[:, bank:bank + 512],
                                         Bb[:, k * TOK + pt * 128:k * TOK + (pt + 1) * 128],
                                         WB[:, (et % 2) * 4096 + k * 512:(et % 2) * 4096 + (k + 1) * 512],
                                         start=(k == 0), stop=(k == NCH - 1))
                        mmw[idx] = c.inc("pe", i)
                        slot_use[et % 2] = mmw[idx]
                    T_(tw)

                    def sw_(s, et=et, pt=pt, idx=idx):
                        c.w(s, "pe", mmw[idx])
                        if idx >= 4:
                            c.w(s, stw[idx - 4][0], stw[idx - 4][1])
                        i = s.activation(out=Fb[:, (idx % 4) * 512:(idx % 4) * 512 + 512],
                                         in_=psA[:, (idx % 2) * 512:(idx % 2) * 512 + 512],
                                         func=AF.Silu, bias=0.0, scale=1.0)
                        assert c.inc("act", i) == evw[idx]
                    wf_sq.append(sw_)

                    def yw(sy, et=et, pt=pt, idx=idx):
                        c.w(sy, "act", evw[idx])
                        i = sy.dma_start(out=dt_["sw_dram"][pt * 128:(pt + 1) * 128, et * 512:(et + 1) * 512],
                                         in_=Fb[:, (idx % 4) * 512:(idx % 4) * 512 + 512])
                        stw[idx] = (f"gs{idx % 4}", c.inc(f"gs{idx % 4}", i, 16))
                    SY_(yw)
            def wflush(s):
                for f in wf_sq:
                    f(s)
            S_(wflush)
            st_deps = [(f"gs{j}", c.cnt.get(f"gs{j}", 0)) for j in range(4)]
            return st_deps, KTP_LD, c.cnt["act"]

        def emit_grid(wfull_st, ktp_ld, wfull_act, lists_done):
            swl = {}
            gmm = {}
            gex = {}
            gstt = {}
            gacc = {}
            gtree = {}
            gmul = {}
            gout = {}
            act0_g = c.cnt["act"]
            for i_ in range(NPT * 8):
                gex[i_] = act0_g + i_ + 1
            grid_sq = []
            for pt in range(NPT):
                def gsw(g, pt=pt):
                    if pt == 0:
                        for nm_, v_ in wfull_st:
                            c.w(g, nm_, v_)
                        c.w(g, "gio", ktp_ld)
                    if pt >= 2:
                        c.w(g, "dve", gmul.get(pt - 2, 0))
                    i = g.dma_start(out=Dv[:, (pt % 2) * 4096:(pt % 2) * 4096 + 4096],
                                    in_=dt_["sw_dram"][pt * 128:(pt + 1) * 128, :])
                    swl[pt] = (f"sl{pt % 2}", c.inc(f"sl{pt % 2}", i, 16))
                G_(gsw)
                for h in range(4):
                    for hb in range(2):
                        sidx = ((pt * 4 + h) * 2 + hb)

                        def tg(t, pt=pt, h=h, hb=hb, sidx=sidx):
                            psS = psA if sidx % 2 == 0 else psB
                            if sidx == 0:
                                c.w(t, "gio", ktp_ld)
                                c.w(t, "act", wfull_act)
                                c.w(t, "dve", lists_done)
                            if sidx > 1:
                                c.w(t, "dve", gstt.get(sidx - 2, 0))
                            for et in range(4):
                                i = t.matmul(psS[:, et * 512:(et + 1) * 512],
                                             Eb[:, h * TOK + pt * 128:h * TOK + (pt + 1) * 128],
                                             C[:, h * 4096 + hb * 2048 + et * 512:h * 4096 + hb * 2048 + (et + 1) * 512],
                                             start=True, stop=True)
                            gmm[sidx] = c.inc("pe", i)
                        T_(tg)

                        def sg(s, pt=pt, h=h, hb=hb, sidx=sidx):
                            psS = psA if sidx % 2 == 0 else psB
                            c.w(s, "pe", gmm[sidx])
                            if sidx < 2:
                                for nm_, v_ in wfull_st:
                                    c.w(s, nm_, v_)
                            if sidx >= 2:
                                c.w(s, "dve", gstt.get(sidx - 1, 0))
                            i = s.activation(out=Fb[:, (sidx % 2) * 2048:(sidx % 2) * 2048 + 2048],
                                             in_=psS[:], func=AF.Exp,
                                             bias=SM[:, 624 + pt * 4 + h:625 + pt * 4 + h], scale=1.0)
                            assert c.inc("act", i) == gex[sidx]
                        grid_sq.append(sg)

                        def vg(v, pt=pt, h=h, hb=hb, sidx=sidx):
                            psS = psA if sidx % 2 == 0 else psB
                            c.w(v, "act", gex[sidx])
                            PBs = [Dv[:, 8192 + hb * 2048:8192 + hb * 2048 + 2048],
                                   Dv[:, 12288 + hb * 2048:12288 + hb * 2048 + 2048],
                                   WB[:, hb * 2048:hb * 2048 + 2048],
                                   WB[:, 4096 + hb * 2048:4096 + hb * 2048 + 2048]]
                            if pt > 0:
                                if h <= 1:
                                    c.w(v, "dve", gtree.get((pt - 1, hb, "t1"), 0))
                                elif (pt - 1, hb, "t2") in gtree:
                                    c.w(v, "pool", gtree[(pt - 1, hb, "t2")])
                            i = v.scalar_tensor_tensor(out=PBs[h], in0=psS[:],
                                                       scalar=SM[:, 560 + pt * 4 + h:561 + pt * 4 + h],
                                                       in1=Fb[:, (sidx % 2) * 2048:(sidx % 2) * 2048 + 2048],
                                                       op0=ALU.is_ge, op1=ALU.mult)
                            gstt[sidx] = c.inc("dve", i)
                        V_(vg)

                for hb in range(2):
                    def vT1(v, pt=pt, hb=hb):
                        c.w(v, "dve", gstt[(pt * 4 + 1) * 2 + hb])
                        c.w(v, "dve", gstt[(pt * 4 + 0) * 2 + hb])
                        i = v.tensor_tensor(out=Dv[:, 8192 + hb * 2048:8192 + hb * 2048 + 2048],
                                            in0=Dv[:, 8192 + hb * 2048:8192 + hb * 2048 + 2048],
                                            in1=Dv[:, 12288 + hb * 2048:12288 + hb * 2048 + 2048],
                                            op=ALU.add)
                        gtree[(pt, hb, "t1a")] = c.inc("dve", i)
                    V_(vT1)

                    def pT2(g, pt=pt, hb=hb):
                        c.w(g, "dve", gstt[(pt * 4 + 3) * 2 + hb])
                        c.w(g, "dve", gstt[(pt * 4 + 2) * 2 + hb])
                        i = g.tensor_tensor(out=WB[:, hb * 2048:hb * 2048 + 2048],
                                            in0=WB[:, hb * 2048:hb * 2048 + 2048],
                                            in1=WB[:, 4096 + hb * 2048:4096 + hb * 2048 + 2048],
                                            op=ALU.add)
                        gtree[(pt, hb, "t2")] = c.inc("pool", i)
                    G_(pT2)

                    def vT3(v, pt=pt, hb=hb):
                        c.w(v, "pool", gtree[(pt, hb, "t2")])
                        c.w(v, "dve", gtree[(pt, hb, "t1a")])
                        i = v.tensor_tensor(out=Dv[:, 8192 + hb * 2048:8192 + hb * 2048 + 2048],
                                            in0=Dv[:, 8192 + hb * 2048:8192 + hb * 2048 + 2048],
                                            in1=WB[:, hb * 2048:hb * 2048 + 2048],
                                            op=ALU.add)
                        gtree[(pt, hb, "t1")] = c.inc("dve", i)
                        c.w(v, swl[pt][0], swl[pt][1])
                        c.w(v, "dve", gtree[(pt, hb, "t1")])
                        if pt >= 1:
                            c.w(v, "gG", gout[pt - 1])
                        i = v.tensor_tensor(out=Fb[:, 4096 + hb * 2048:4096 + hb * 2048 + 2048],
                                            in0=Dv[:, 8192 + hb * 2048:8192 + hb * 2048 + 2048],
                                            in1=Dv[:, (pt % 2) * 4096 + hb * 2048:(pt % 2) * 4096 + hb * 2048 + 2048],
                                            op=ALU.mult)
                        gmul[pt] = c.inc("dve", i)
                    V_(vT3)

                def yG(sy, pt=pt):
                    c.w(sy, "dve", gmul[pt])
                    i = sy.dma_start(out=dt_["g_dram"][pt * 128:(pt + 1) * 128, :], in_=Fb[:, 4096:8192])
                    gout[pt] = c.inc("gG", i, 16)
                SY_(yG)

            def gflush(s):
                for f in grid_sq:
                    f(s)
            S_(gflush)
            return c.cnt["gG"]

        def emit_swiglu(xn_ready, grid_deps):
            act0_s = c.cnt["act"]
            evg = {}
            evu = {}
            evm = {}
            for p2 in range(2):
                base = act0_s + p2 * 144
                for dsl in range(8):
                    for msub in range(4):
                        for nh in range(2):
                            k_ = dsl * 8 + msub * 2 + nh
                            evg[(p2, k_)] = base + k_ * 2 + 1
                            evu[(p2, k_)] = base + k_ * 2 + 2
                for hf in range(2):
                    for dc in range(8):
                        evm[(p2, hf, dc)] = base + 128 + hf * 8 + dc + 1
            sw_sq = []
            wldg = {}
            wldu = {}
            wldd = {}
            mmg = {}
            mmu = {}
            hml = {}
            mmm = {}
            for p2 in range(2):
                for dsl in range(8):
                    key = p2 * 8 + dsl

                    def gw(g, p2=p2, dsl=dsl, key=key):
                        if key == 0:
                            for nm_, v_ in grid_deps:
                                c.w(g, nm_, v_)
                        par = dsl % 2
                        if key >= 2:
                            c.w(g, "pe", mmu.get(key - 2, 0))
                        i = g.dma_start(out=WB[:, par * 4096:par * 4096 + 4096]
                                        .rearrange("p (k o) -> p k o", k=8),
                                        in_=dt_["wg"][:, dsl * 512:(dsl + 1) * 512]
                                        .rearrange("(k p) o -> p k o", p=128))
                        wldg[key] = (f"sg{par}", c.inc(f"sg{par}", i, 16))
                        i = g.dma_start(out=Eb[:, par * 4096:par * 4096 + 4096]
                                        .rearrange("p (k o) -> p k o", k=8),
                                        in_=dt_["wu"][:, dsl * 512:(dsl + 1) * 512]
                                        .rearrange("(k p) o -> p k o", p=128))
                        wldu[key] = (f"su{par}", c.inc(f"su{par}", i, 16))
                    G_(gw)

                    for msub in range(4):
                        for nh in range(2):
                            k_ = dsl * 8 + msub * 2 + nh
                            par = dsl % 2
                            bcol = dsl * 4 + msub
                            tt = p2 * 2 + nh

                            def tg_(t, p2=p2, dsl=dsl, key=key, msub=msub, nh=nh, k_=k_, par=par, tt=tt):
                                if k_ == 0:
                                    if p2 == 0:
                                        c.w(t, "dve", xn_ready)
                                    else:
                                        c.w(t, "act", evm[(p2 - 1, 1, 7)])
                                c.w(t, wldg[key][0], wldg[key][1])
                                if k_ >= 2:
                                    c.w(t, "act", evg.get((p2, k_ - 2), 0))
                                bank = (k_ % 2) * 512
                                for k in range(NCH):
                                    i = t.matmul(psA[:, bank:bank + 512],
                                                 WB[:, par * 4096 + k * 512 + msub * 128:par * 4096 + k * 512 + (msub + 1) * 128],
                                                 Bb[:, k * TOK + tt * 512:k * TOK + (tt + 1) * 512],
                                                 start=(k == 0), stop=(k == NCH - 1))
                                mmg[(p2, k_)] = c.inc("pe", i)
                                c.w(t, wldu[key][0], wldu[key][1])
                                if k_ >= 2:
                                    c.w(t, "act", evu.get((p2, k_ - 2), 0))
                                for k in range(NCH):
                                    i = t.matmul(psA[:, 1024 + bank:1024 + bank + 512],
                                                 Eb[:, par * 4096 + k * 512 + msub * 128:par * 4096 + k * 512 + (msub + 1) * 128],
                                                 Bb[:, k * TOK + tt * 512:k * TOK + (tt + 1) * 512],
                                                 start=(k == 0), stop=(k == NCH - 1))
                                mmu[key] = c.inc("pe", i)
                                mmg[(p2, k_, "u")] = mmu[key]
                            T_(tg_)

                            def sv_(s, p2=p2, k_=k_, bcol=bcol):
                                bank = (k_ % 2) * 512
                                c.w(s, "pe", mmg[(p2, k_, "u")])
                                if k_ >= 2:
                                    c.w(s, "dve", hml.get((p2, k_ - 2), 0))
                                i = s.activation(out=Fb[:, (k_ % 2) * 512:(k_ % 2) * 512 + 512],
                                                 in_=psA[:, bank:bank + 512], func=AF.Silu,
                                                 bias=BGU[:, bcol:bcol + 1], scale=1.0)
                                assert c.inc("act", i) == evg[(p2, k_)]
                                i = s.activation(out=Fb[:, 1024 + (k_ % 2) * 512:1024 + (k_ % 2) * 512 + 512],
                                                 in_=psA[:, 1024 + bank:1024 + bank + 512], func=AF.Identity,
                                                 bias=BGU[:, 32 + bcol:33 + bcol], scale=1.0)
                                assert c.inc("act", i) == evu[(p2, k_)]
                            sw_sq.append(sv_)

                            def vh_(v, p2=p2, dsl=dsl, msub=msub, nh=nh, k_=k_):
                                c.w(v, "act", evu[(p2, k_)])
                                kc = dsl * 4 + msub
                                dst = (C[:, kc * 1024 + nh * 512:kc * 1024 + nh * 512 + 512] if kc < 16
                                       else Dv[:, (kc - 16) * 1024 + nh * 512:(kc - 16) * 1024 + nh * 512 + 512])
                                i = v.tensor_tensor(out=dst,
                                                    in0=Fb[:, (k_ % 2) * 512:(k_ % 2) * 512 + 512],
                                                    in1=Fb[:, 1024 + (k_ % 2) * 512:1024 + (k_ % 2) * 512 + 512],
                                                    op=ALU.mult)
                                hml[(p2, k_)] = c.inc("dve", i)
                            V_(vh_)

                for hf in range(2):
                    tt = p2 * 2 + hf
                    for kc in range(32):
                        dkey = (p2 * 2 + hf) * 32 + kc

                        def gwd(g, p2=p2, hf=hf, kc=kc, dkey=dkey):
                            if dkey >= 4:
                                c.w(g, "pe", mmm.get(dkey - 4, 0))
                            i = g.dma_start(out=Fb[:, 4096 + (kc % 4) * 1024:4096 + (kc % 4) * 1024 + 1024],
                                            in_=dt_["wd"][kc * 128:(kc + 1) * 128, :])
                            wldd[dkey] = (f"sd{kc % 4}", c.inc(f"sd{kc % 4}", i, 16))
                        G_(gwd)

                        def tm_(t, p2=p2, hf=hf, tt=tt, kc=kc, dkey=dkey):
                            c.w(t, wldd[dkey][0], wldd[dkey][1])
                            if kc == 0:
                                c.w(t, "dve", hml[(p2, 63)])
                                if hf == 0 and p2 == 0:
                                    pass
                                c.w(t, "act", evm.get((p2, hf - 1, 7), evm.get((p2 - 1, 1, 7), 0)) if hf or p2 else 0)
                            for dc in range(NCH):
                                ps_ = psA if dc < 4 else psB
                                i = t.matmul(ps_[:, (dc % 4) * 512:(dc % 4) * 512 + 512],
                                             Fb[:, 4096 + (kc % 4) * 1024 + dc * 128:4096 + (kc % 4) * 1024 + (dc + 1) * 128],
                                             C[:, kc * 1024 + hf * 512:kc * 1024 + hf * 512 + 512] if kc < 16
                                             else Dv[:, (kc - 16) * 1024 + hf * 512:(kc - 16) * 1024 + hf * 512 + 512],
                                             start=(kc == 0), stop=(kc == 31))
                            mmm[dkey] = c.inc("pe", i)
                        T_(tm_)

                    def sm_(s, p2=p2, hf=hf, tt=tt):
                        c.w(s, "pe", mmm[(p2 * 2 + hf) * 32 + 31])
                        for dc in range(NCH):
                            ps_ = psA if dc < 4 else psB
                            i = s.activation(out=A[:, dc * TOK + tt * 512:dc * TOK + (tt + 1) * 512],
                                             in_=ps_[:, (dc % 4) * 512:(dc % 4) * 512 + 512],
                                             func=AF.Identity, bias=0.0, scale=1.0)
                            assert c.inc("act", i) == evm[(p2, hf, dc)]
                    sw_sq.append(sm_)

            def swflush(s):
                for f in sw_sq:
                    f(s)
            S_(swflush)
            return c.cnt["act"], c.cnt["pe"], c.cnt["dve"]

        def emit_up(swiglu_done, grid_G_done):
            sw_act, sw_pe, sw_dve = swiglu_done

            def gup(g):
                c.w(g, "pe", sw_pe)
                for ec in range(32):
                    dst = C[:, ec * 1024:(ec + 1) * 1024] if ec < 16 else Dv[:, (ec - 16) * 1024:(ec - 15) * 1024]
                    i = g.dma_start(out=dst, in_=dt_["up_nat"][ec * 128:(ec + 1) * 128, :])
                    c.inc("upl", i, 16)
            G_(gup)
            UPL = c.cnt["upl"]

            gtl = {}
            mmup = {}
            cmb = {}
            g1ev = {}
            g1st = {}
            act0_u = c.cnt["act"]
            for i_ in range(32):
                g1ev[i_] = act0_u + i_ + 1
            up_sq = []
            for tt in range(NT4):
                def ygt(sy, tt=tt):
                    if tt == 0:
                        c.w(sy, "gG", grid_G_done)
                        c.w(sy, "pe", sw_pe)
                        c.w(sy, "act", sw_act)
                    if tt >= 1:
                        c.w(sy, "pe", mmup.get((tt - 1) * 8 + 7, 0))
                    for ec in range(32):
                        dst = Eb[:, ec * 512:(ec + 1) * 512] if ec < 16 else Fb[:, (ec - 16) * 512:(ec - 15) * 512]
                        i = sy.dma_start_transpose(out=dst,
                                                   in_=dt_["g_dram"][tt * 512:(tt + 1) * 512, ec * 128:(ec + 1) * 128])
                        c.inc("gt", i, 16)
                    gtl[tt] = c.cnt["gt"]
                SY_(ygt)

                for dc in range(NCH):
                    ukey = tt * 8 + dc

                    def tu(t, tt=tt, dc=dc, ukey=ukey):
                        if ukey == 0:
                            c.w(t, "upl", UPL)
                            c.w(t, "act", sw_act)
                        c.w(t, "gt", gtl[tt])
                        if ukey >= 2:
                            c.w(t, "dve", cmb.get(ukey - 2, 0))
                        bank = (ukey % 2) * 512
                        for ec in range(32):
                            lhs = C[:, ec * 1024 + dc * 128:ec * 1024 + (dc + 1) * 128] if ec < 16                                 else Dv[:, (ec - 16) * 1024 + dc * 128:(ec - 16) * 1024 + (dc + 1) * 128]
                            rhs = Eb[:, ec * 512:(ec + 1) * 512] if ec < 16 else Fb[:, (ec - 16) * 512:(ec - 15) * 512]
                            i = t.matmul(psA[:, bank:bank + 512], lhs, rhs,
                                         start=(ec == 0), stop=(ec == 31))
                        mmup[ukey] = c.inc("pe", i)
                    T_(tu)

                    def vu(v, tt=tt, dc=dc, ukey=ukey):
                        c.w(v, "pe", mmup[ukey])
                        if ukey == 0:
                            c.w(v, "act", sw_act)
                        if ukey >= 2:
                            c.w(v, "act", g1ev[ukey - 2])
                        bank = (ukey % 2) * 512
                        i = v.scalar_tensor_tensor(out=SCR[:, (ukey % 2) * 512:(ukey % 2) * 512 + 512],
                                                   in0=psA[:, bank:bank + 512],
                                                   scalar=CB[:, 57 + dc:58 + dc],
                                                   in1=A[:, dc * TOK + tt * 512:dc * TOK + (tt + 1) * 512],
                                                   op0=ALU.add, op1=ALU.add)
                        cmb[ukey] = c.inc("dve", i)
                    V_(vu)

                    def su(s, tt=tt, dc=dc, ukey=ukey):
                        c.w(s, "dve", cmb[ukey])
                        if ukey >= 4:
                            c.w(s, g1st[ukey - 4][0], g1st[ukey - 4][1])
                        i = s.activation(out=WB[:, (ukey % 4) * 512:(ukey % 4) * 512 + 512],
                                         in_=SCR[:, (ukey % 2) * 512:(ukey % 2) * 512 + 512],
                                         func=AF.Silu, bias=0.0, scale=1.0)
                        assert c.inc("act", i) == g1ev[ukey]
                    up_sq.append(su)

                    def yu(sy, tt=tt, dc=dc, ukey=ukey):
                        c.w(sy, "act", g1ev[ukey])
                        i = sy.dma_start(out=dt_["g_act"][dc * 128:(dc + 1) * 128, tt * 512:(tt + 1) * 512],
                                         in_=WB[:, (ukey % 4) * 512:(ukey % 4) * 512 + 512])
                        g1st[ukey] = (f"g1{ukey % 4}", c.inc(f"g1{ukey % 4}", i, 16))
                    SY_(yu)
            def uflush(s):
                for f in up_sq:
                    f(s)
            S_(uflush)
            st_deps = [(f"g1{j}", c.cnt.get(f"g1{j}", 0)) for j in range(4)]
            return st_deps, c.cnt["pe"], c.cnt["dve"], c.cnt["act"]

        PH = upto
        if PH >= 1:
            TERN0 = emit_l0_tern()
        ln0 = emit_ln(dt_["xT"], prev_done=[("dve", c.cnt["dve"] if PH >= 1 else 0)])
        if "t_xn" in taps:
            tap_fm("t_xn", Bb, [("dve", ln0)])
        if PH >= 1:
            ffn0 = emit_l0_ffn_mm(ln0, TERN0)
        if PH >= 2:
            qkv_prev = [("pe", ffn0[1]), ("act", ffn0[2])] if PH >= 1 else [("dve", ln0)]
            qkv0 = emit_qkv(0, ln0, qkv_prev)
            if "t_qn" in taps:
                tap_fm("t_qn", A, [("dve", qkv0[0]), ("act", qkv0[1])])
            if "t_kn" in taps:
                tap_fm("t_kn", C, [("dve", qkv0[0]), ("act", qkv0[1])])
            if "t_v" in taps:
                tapout("t_v", Dv[:], deps=[("act", qkv0[1]), ("dve", qkv0[0])])
        if PH >= 3:
            at0 = emit_attn(0, [("dve", qkv0[0]), ("act", qkv0[1]), ("gst", c.cnt["gst"])])
            if "t_attn" in taps:
                tap_fm("t_attn", A, [("dve", at0)])
        if PH >= 4:
            pr0 = emit_proj_residual(0, at0)

        if PH >= 5:
            _allsems = [(nm, c.cnt[nm]) for nm in sorted(c.cnt.keys()) if nm not in ("pe", "act", "dve", "pool")]
            ln1 = emit_ln(dt_["x_mid"], prev_done=_allsems)
            if "t_xn1" in taps:
                tap_fm("t_xn1", Bb, [("dve", ln1)])
        if PH >= 6:
            qpk = emit_qpk(ln1, [("dve", ln1)])
            if "t_qpk" in taps:
                def _tq(g):
                    c.w(g, "act", qpk)
                    i = g.dma_start(out=dt_["t_qpk"].rearrange("(h p) t -> p h t", p=128),
                                    in_=Eb[:].rearrange("p (h t) -> p h t", h=4))
                    c.inc("gst", i, 16)
                G_(_tq)
        if PH >= 7:
            wf_st, ktp_ld, wf_act = emit_wfull(ln1)
            lst = emit_lists(qpk)
            if "t_tau" in taps:
                def _tt(g):
                    c.w(g, "dve", lst)
                    i = g.dma_start(out=dt_["t_tau"], in_=SM[:, 560:624])
                    c.inc("gst", i, 16)
                G_(_tt)
            if "t_negc" in taps:
                def _tn(g):
                    c.w(g, "dve", lst)
                    i = g.dma_start(out=dt_["t_negc"], in_=SM[:, 624:688])
                    c.inc("gst", i, 16)
                G_(_tn)
        if PH >= 8:
            grid_G = emit_grid(wf_st, ktp_ld, wf_act, lst)
        if PH >= 9:
            swg = emit_swiglu(ln1, [("dve", c.cnt["dve"]), ("act", c.cnt["act"]), ("gG", c.cnt.get("gG", 0)), ("gst", c.cnt["gst"]), ("pool", c.cnt["pool"])])
            if "t_mlp" in taps:
                tap_fm("t_mlp", A, [("act", swg[0])])
        if PH >= 10:
            upres = emit_up(swg, grid_G)
        if PH >= 11:
            prev1 = [("pe", upres[1]), ("dve", upres[2]), ("gst", c.cnt["gst"])] + upres[0]
            qkv1 = emit_qkv(1, ln1, prev1)
        if PH >= 12:
            at1 = emit_attn(1, [("dve", qkv1[0]), ("act", qkv1[1]), ("gst", c.cnt["gst"])])
            if "t_attn1" in taps:
                tap_fm("t_attn1", A, [("dve", at1)])
        if PH >= 13:
            pr1 = emit_proj_residual(1, at1, extra_deps=upres[0])

        def fin(g):
            for nm in sorted(c.cnt.keys()):
                c.w(g, nm)
        G_(fin)

    return nc, dt_



def _prep_core_inputs(core, full):
    f32 = np.float32
    out = {"xT": np.ascontiguousarray(full["L0_attn_x"][core].T, dtype=f32)}
    out["l0_wqkvT"] = np.ascontiguousarray(full["L0_attn_Wqkv"].T, dtype=f32)
    out["l0_bqkv_col"] = np.ascontiguousarray(full["L0_attn_bqkv"].reshape(24, 128).T, dtype=f32)
    out["l0_bqkv_vrow"] = np.ascontiguousarray(full["L0_attn_bqkv"][2048:3072].reshape(1, 1024), dtype=f32)
    out["l0_wprojT"] = np.ascontiguousarray(full["L0_attn_Wproj"].T, dtype=f32)
    out["l0_bproj_col"] = np.ascontiguousarray(full["L0_attn_bproj"].reshape(8, 128).T, dtype=f32)
    out["qpat_col"] = np.ascontiguousarray(np.tile(np.asarray(full["L0_attn_qn_w"]), 16).reshape(8, 128).T, dtype=f32)
    out["kpat_col"] = np.ascontiguousarray(np.tile(np.asarray(full["L0_attn_kn_w"]), 16).reshape(8, 128).T, dtype=f32)
    out["l0_ffnT"] = np.ascontiguousarray(full["L0_ffn_weight"].T, dtype=f32)
    out["alpha_col"] = np.ascontiguousarray(np.broadcast_to(np.asarray(full["L0_ffn_alpha"]).reshape(1, 1), (128, 1)), dtype=f32)
    out["l0_fbias_col"] = np.ascontiguousarray(np.asarray(full["L0_ffn_bias"]).reshape(8, 128).T, dtype=f32)
    k1 = np.asarray(full["L1_attn_kernel"])
    out["l1_wqkvT"] = np.ascontiguousarray(np.transpose(k1, (2, 0, 1, 3)).reshape(1024, 3072), dtype=f32)
    wo = np.asarray(full["L1_attn_WO"])[0]
    out["l1_woT"] = np.ascontiguousarray(wo.reshape(64, 16, 1024).transpose(1, 0, 2).reshape(1024, 1024), dtype=f32)
    wq = np.asarray(full["L1_ffn_Wq"])
    out["wq_p"] = np.ascontiguousarray(wq.reshape(1024, 2, 4, 64).transpose(0, 2, 1, 3).reshape(1024, 512), dtype=f32)
    keys = np.asarray(full["L1_ffn_keys"])
    out["keysP"] = np.ascontiguousarray(keys.transpose(0, 2, 3, 1).reshape(512, 64), dtype=f32)
    kx = keys[:, :, 0, :]
    ky = keys[:, :, 1, :]
    ktp = np.empty((4, 128, NEXP), dtype=f32)
    for h in range(4):
        ktp[h, 0:64] = np.repeat(kx[h].T, 64, axis=1)
        ktp[h, 64:128] = np.tile(ky[h].T, (1, 64))
    out["ktp"] = np.ascontiguousarray(ktp.reshape(512, NEXP))
    out["down_T"] = np.ascontiguousarray(np.asarray(full["L1_ffn_down_embed"]).T, dtype=f32)
    out["up_nat"] = np.ascontiguousarray(np.asarray(full["L1_ffn_up_embed"]), dtype=f32)
    out["wg"] = np.ascontiguousarray(np.asarray(full["L1_ffn_Wg"]), dtype=f32)
    out["wu"] = np.ascontiguousarray(np.asarray(full["L1_ffn_Wu"]), dtype=f32)
    out["wd"] = np.ascontiguousarray(np.asarray(full["L1_ffn_Wd"]), dtype=f32)
    out["bg_col"] = np.ascontiguousarray(np.asarray(full["L1_ffn_bg"]).reshape(32, 128).T, dtype=f32)
    out["bu_col"] = np.ascontiguousarray(np.asarray(full["L1_ffn_bu"]).reshape(32, 128).T, dtype=f32)
    out["bd_col"] = np.ascontiguousarray(np.asarray(full["L1_ffn_bd"]).reshape(8, 128).T, dtype=f32)
    blk2 = np.zeros((128, 2), dtype=f32)
    blk2[0:64, 0] = 1.0
    blk2[64:128, 1] = 1.0
    out["blk2"] = blk2
    epscol = np.zeros((128, 4), dtype=f32)
    epscol[:, 0] = LN_EPS
    epscol[:, 1] = RMS_EPS
    epscol[:, 2] = 1e-30
    out["epscol"] = epscol
    sel2 = np.zeros((2, 128), dtype=f32)
    sel2[0, 0:64] = 1.0
    sel2[1, 64:128] = 1.0
    out["sel2"] = sel2
    return out


def kernel(**inputs):
    nc, dt_ = build_program()
    in_maps = [_prep_core_inputs(core, inputs) for core in range(8)]
    res = run_bass_kernel_spmd(nc, in_maps, core_ids=list(range(8)))
    outs = [np.asarray(r["outT"], dtype=np.float32).T for r in res.results]
    return np.stack(outs, axis=0)


# version 21
# speedup vs baseline: 1.1910x; 1.1910x over previous
import numpy as np
import ml_dtypes
from contextlib import ExitStack
import concourse.bass as bass
import concourse.mybir as mybir
from concourse.bass_utils import run_bass_kernel_spmd

F32 = mybir.dt.float32
BF16 = mybir.dt.bfloat16
AF = mybir.ActivationFunctionType
ALU = mybir.AluOpType
AX = mybir.AxisListType.X

D = 1024
NHEAD = 16
HD = 64
NCH = 8
TOK = 2048
NT4 = TOK // 512
NPT = TOK // 128
LN_EPS = 1e-5
RMS_EPS = 1e-6
L0_SCALE = 0.125
L1_SCALE = 1.0 / 65536.0
NEXP = 4096
DCD = 4096


class Ctx:
    def __init__(self, nc, es):
        self.nc = nc
        self.es = es
        self.sem = {}
        self.cnt = {}
        for name in ["gio", "gst", "sy", "pe", "act", "dve", "pool"]:
            self._mk(name)

    def _mk(self, name):
        self.sem[name] = self.es.enter_context(self.nc.semaphore(name))
        self.cnt[name] = 0

    def inc(self, name, inst, k=1):
        if name not in self.sem:
            self._mk(name)
        self.cnt[name] += k
        inst.then_inc(self.sem[name], k)
        return self.cnt[name]

    def w(self, eng, name, val=None):
        if name not in self.sem:
            self._mk(name)
        v = self.cnt[name] if val is None else val
        if v > 0:
            eng.wait_ge(self.sem[name], v)


def build_program(upto=99, taps=(), dev_xmid=False, dev_expose=False):
    nc = bass.Bass("TRN2", target_bir_lowering=False)
    dt_ = {}

    def din(name, shape, dtype=F32):
        dt_[name] = nc.dram_tensor(name, shape, dtype, kind="ExternalInput").ap()

    def dou(name, shape, dtype=F32):
        dt_[name] = nc.dram_tensor(name, shape, dtype, kind="ExternalOutput").ap()

    def dit(name, shape, dtype):
        dt_[name] = nc.dram_tensor(name, shape, dtype, kind="Internal").ap()

    din("xT", [D, TOK])
    din("l0_wqkvT", [D, 3 * D]); din("l0_bqkv_col", [128, 24]); din("l0_bqkv_vrow", [1, D])
    din("l0_wprojT", [D, D]); din("l0_bproj_col", [128, 8])
    din("qpat_col", [128, 8]); din("kpat_col", [128, 8])
    din("l0_ffnT", [D, D]); din("alpha_col", [128, 1]); din("l0_fbias_col", [128, 8])
    din("l1_wqkvT", [D, 3 * D]); din("l1_woT", [D, D])
    din("wq_p", [D, 512]); din("keysP", [512, 64]); din("ktp", [512, NEXP])
    din("down_T", [D, NEXP]); din("up_nat", [NEXP, D])
    din("wg", [D, DCD]); din("wu", [D, DCD]); din("wd", [DCD, D])
    din("bg_col", [128, 32]); din("bu_col", [128, 32]); din("bd_col", [128, 8])
    din("blk2", [128, 2]); din("sel2", [2, 128]); din("epscol", [128, 4])
    (dou if (dev_xmid or dev_expose) else dit)("x_mid", [D, TOK], BF16)
    (dou if dev_expose else dit)("g_act", [D, TOK], BF16)
    (dou if dev_expose else dit)("sw_dram", [TOK, NEXP], BF16)
    (dou if dev_expose else dit)("g_dram", [TOK, NEXP], BF16)
    dou("outT", [D, TOK])
    for t in taps:
        shp = {"t_xn": [D, TOK], "t_qn": [D, TOK], "t_kn": [D, TOK], "t_v": [128, 1040 * NPT],
               "t_attn": [D, TOK], "t_x1": [D, TOK], "t_g": [D, TOK],
               "t_qpk": [512, TOK], "t_sims": [TOK, 512], "t_tau": [128, 64], "t_negc": [128, 64],
               "t_sw": [TOK, NEXP], "t_gg": [TOK, NEXP], "t_mlp": [D, TOK], "t_g1": [D, TOK],
               "t_xn1": [D, TOK], "t_attn1": [D, TOK]}[t]
        dou(t, shp)

    es = ExitStack()
    with es:
        E = es.enter_context
        A = E(nc.sbuf_tensor("A", [128, 8 * TOK], BF16))
        Bb = E(nc.sbuf_tensor("Bb", [128, 8 * TOK], BF16))
        C = E(nc.sbuf_tensor("C", [128, 8 * TOK], BF16))
        Dv = E(nc.sbuf_tensor("Dv", [128, 1040 * NPT], BF16))
        Eb = E(nc.sbuf_tensor("Eb", [128, 4 * TOK], BF16))
        Fb = E(nc.sbuf_tensor("Fb", [128, 4 * TOK], BF16))
        WB = E(nc.sbuf_tensor("WB", [128, 10240], BF16))
        SCR = E(nc.sbuf_tensor("SCR", [128, 4096], F32))
        SM = E(nc.sbuf_tensor("SM", [128, 1280], F32))
        ONESbr = E(nc.sbuf_tensor("ONESbr", [1, 128], BF16))
        ONESfr = E(nc.sbuf_tensor("ONESfr", [1, 128], F32))
        ONESfc = E(nc.sbuf_tensor("ONESfc", [128, 1], F32))
        ONESbc = E(nc.sbuf_tensor("ONESbc", [128, 1], BF16))
        CB = E(nc.sbuf_tensor("CB", [128, 80], F32))
        BGU = E(nc.sbuf_tensor("BGU", [128, 64], F32))
        SEL = E(nc.sbuf_tensor("SEL", [2, 128], F32))
        BLK2B = E(nc.sbuf_tensor("BLK2B", [128, 2], BF16))
        KEYSP = E(nc.sbuf_tensor("KEYSP", [128, 256], BF16))
        BROW = E(nc.sbuf_tensor("BROW", [1, D], BF16))
        psA = E(nc.psum_tensor("psA", [128, 2048], F32))
        psB = E(nc.psum_tensor("psB", [128, 2048], F32))
        blk = E(nc.Block())
        c = Ctx(nc, es)
        T_, S_, V_, G_, SY_ = blk.tensor, blk.scalar, blk.vector, blk.gpsimd, blk.sync

        def _(g):
            ii = [g.dma_start(out=CB[:, 0:24], in_=dt_["l0_bqkv_col"]),
                  g.dma_start(out=CB[:, 24:32], in_=dt_["l0_bproj_col"]),
                  g.dma_start(out=CB[:, 32:40], in_=dt_["qpat_col"]),
                  g.dma_start(out=CB[:, 40:48], in_=dt_["kpat_col"]),
                  g.dma_start(out=CB[:, 48:49], in_=dt_["alpha_col"]),
                  g.dma_start(out=CB[:, 49:57], in_=dt_["l0_fbias_col"]),
                  g.dma_start(out=CB[:, 57:65], in_=dt_["bd_col"]),
                  g.dma_start(out=CB[:, 65:67], in_=dt_["blk2"]),
                  g.dma_start(out=BLK2B[:], in_=dt_["blk2"]),
                  g.dma_start(out=CB[:, 67:71], in_=dt_["epscol"]),
                  g.dma_start(out=SEL[:], in_=dt_["sel2"]),
                  g.dma_start(out=KEYSP[:].rearrange("p (h k) -> p h k", h=4), in_=dt_["keysP"].rearrange("(h p) k -> p h k", p=128)),
                  g.dma_start(out=BROW[:], in_=dt_["l0_bqkv_vrow"]),
                  g.dma_start(out=BGU[:, 0:32], in_=dt_["bg_col"]),
                  g.dma_start(out=BGU[:, 32:64], in_=dt_["bu_col"])]
            for i in ii:
                c.inc("gio", i, 16)
            c.inc("pool", g.memset(ONESbr[:], 1.0))
            c.inc("pool", g.memset(ONESfr[:], 1.0))
            c.inc("pool", g.memset(ONESfc[:], 1.0))
            c.inc("pool", g.memset(ONESbc[:], 1.0))
        G_(_)
        CONST_GIO = c.cnt["gio"]
        CONST_POOL = c.cnt["pool"]

        def tapout(name, src_ap, rows=None, deps=()):
            if name not in taps:
                return
            def _(g):
                for nm, v in deps:
                    c.w(g, nm, v)
                i = g.dma_start(out=dt_[name] if rows is None else dt_[name][rows[0]:rows[1], :],
                                in_=src_ap)
                c.inc("gst", i, 16)
            G_(_)

        def emit_ln(x_src, prev_done=()):
            def gl(g):
                for nm, v_ in prev_done:
                    c.w(g, nm, v_)
                i = g.dma_start(out=A[:].rearrange("p (ch t) -> p ch t", ch=NCH),
                                in_=x_src.rearrange("(ch p) t -> p ch t", p=128))
                c.inc("gio", i, 16)
            G_(gl)
            LD = c.cnt["gio"]

            sqd = {}
            mmd = {}
            for ch in range(NCH):
                def s1(s, ch=ch):
                    if ch == 0:
                        c.w(s, "gio", LD)
                    if ch >= 2:
                        c.w(s, "pe", mmd[ch - 2])
                    i = s.activation(out=SCR[:, (ch % 2) * 2048:(ch % 2) * 2048 + 2048],
                                     in_=A[:, ch * TOK:(ch + 1) * TOK], func=AF.Square, bias=0.0, scale=1.0)
                    sqd[ch] = c.inc("act", i)
                S_(s1)

                def t1(t, ch=ch):
                    if ch == 0:
                        c.w(t, "gio", LD)
                        c.w(t, "pool", CONST_POOL)
                    for nt in range(NT4):
                        t.matmul(psB[0:1, nt * 512:(nt + 1) * 512], ONESbc[:],
                                 A[:, ch * TOK + nt * 512:ch * TOK + (nt + 1) * 512],
                                 start=(ch == 0), stop=(ch == NCH - 1))
                    c.w(t, "act", sqd[ch])
                    for nt in range(NT4):
                        i = t.matmul(psA[0:1, nt * 512:(nt + 1) * 512], ONESfc[:],
                                     SCR[:, (ch % 2) * 2048 + nt * 512:(ch % 2) * 2048 + (nt + 1) * 512],
                                     start=(ch == 0), stop=(ch == NCH - 1))
                    mmd[ch] = c.inc("pe", i)
                T_(t1)
            MM = c.cnt["pe"]

            def s2(s):
                c.w(s, "pe", MM)
                i = s.activation(out=SCR[0:1, 0:2048], in_=psB[0:1, :], func=AF.Identity, bias=0.0, scale=1.0 / D)
                c.inc("act", i)
                i = s.activation(out=SCR[64:65, 2048:4096], in_=psA[0:1, :], func=AF.Identity, bias=0.0, scale=1.0 / D)
                c.inc("act", i)
            S_(s2)
            A1 = c.cnt["act"]

            def v1(v):
                c.w(v, "act", A1)
                i = v.tensor_tensor(out=SCR[64:65, 0:2048], in0=SCR[0:1, 0:2048], in1=SCR[0:1, 0:2048], op=ALU.mult)
                k = c.inc("dve", i)
                c.w(v, "dve", k)
                i = v.tensor_tensor(out=SCR[32:33, 2048:4096], in0=SCR[64:65, 2048:4096], in1=SCR[64:65, 0:2048], op=ALU.subtract)
                c.inc("dve", i)
            V_(v1)
            D1 = c.cnt["dve"]

            def s3(s):
                c.w(s, "dve", D1)
                i = s.activation(out=SCR[32:33, 0:2048], in_=SCR[32:33, 2048:4096], func=AF.Ln, bias=CB[0:1, 67:68], scale=1.0)
                k = c.inc("act", i)
                c.w(s, "act", k)
                i = s.activation(out=SCR[0:1, 2048:4096], in_=SCR[32:33, 0:2048], func=AF.Exp, bias=0.0, scale=-0.5)
                c.inc("act", i)
            S_(s3)
            A2 = c.cnt["act"]

            def v2(v):
                c.w(v, "act", A2)
                i = v.tensor_tensor(out=SCR[32:33, 0:2048], in0=SCR[0:1, 0:2048], in1=SCR[0:1, 2048:4096], op=ALU.mult)
                c.inc("dve", i)
            V_(v2)
            D2a = c.cnt["dve"]

            def s3b(s):
                c.w(s, "dve", D2a)
                i = s.activation(out=SCR[0:1, 0:2048], in_=SCR[32:33, 0:2048], func=AF.Identity, bias=0.0, scale=1.0)
                c.inc("act", i)
            S_(s3b)
            A2b = c.cnt["act"]

            def v2b(v):
                c.w(v, "act", A2b)
                i = v.engine_nop() if False else None
            D2 = ("act", A2b)

            def t2(t):
                c.w(t, "act", A2b)
                for nt in range(NT4):
                    t.matmul(psA[:, nt * 512:(nt + 1) * 512], ONESfr[:], SCR[0:1, 2048 + nt * 512:2048 + (nt + 1) * 512],
                             start=True, stop=True)
                for nt in range(NT4):
                    i = t.matmul(psB[:, nt * 512:(nt + 1) * 512], ONESfr[:], SCR[0:1, nt * 512:(nt + 1) * 512],
                                 start=True, stop=True)
                c.inc("pe", i)
            T_(t2)
            BC = c.cnt["pe"]

            def s4(s):
                c.w(s, "pe", BC)
                i = s.activation(out=SCR[:, 0:2048], in_=psA[:], func=AF.Identity, bias=0.0, scale=1.0)
                c.inc("act", i)
                i = s.activation(out=SCR[:, 2048:4096], in_=psB[:], func=AF.Identity, bias=0.0, scale=1.0)
                c.inc("act", i)
            S_(s4)
            A3 = c.cnt["act"]

            def v3(v):
                c.w(v, "act", A3)
                for ch in range(NCH):
                    for hf in range(2):
                        pb = (ch * 2 + hf) % 2
                        c.w(v, "dve", c.cnt["dve"])
                        i = v.tensor_tensor(out=psA[:, pb * 1024:pb * 1024 + 1024],
                                            in0=A[:, ch * TOK + hf * 1024:ch * TOK + hf * 1024 + 1024],
                                            in1=SCR[:, hf * 1024:hf * 1024 + 1024], op=ALU.mult)
                        k = c.inc("dve", i)
                        c.w(v, "dve", k)
                        i = v.tensor_tensor(out=Bb[:, ch * TOK + hf * 1024:ch * TOK + hf * 1024 + 1024],
                                            in0=psA[:, pb * 1024:pb * 1024 + 1024],
                                            in1=SCR[:, 2048 + hf * 1024:2048 + hf * 1024 + 1024], op=ALU.subtract)
                        c.inc("dve", i)
            V_(v3)
            return c.cnt["dve"]

        def tap_fm(name, buf, deps):
            if name not in taps:
                return
            def _(g):
                for nm, v in deps:
                    c.w(g, nm, v)
                i = g.dma_start(out=dt_[name].rearrange("(ch p) t -> p ch t", p=128),
                                in_=buf[:].rearrange("p (ch t) -> p ch t", ch=NCH))
                c.inc("gst", i, 16)
            G_(_)

        def emit_l0_tern():
            ld1 = {}
            red = {}
            for ch in range(NCH):
                def gl(g, ch=ch):
                    if ch >= 2:
                        c.w(g, "dve", red[ch - 2])
                    i = g.dma_start(out=SCR[:, (ch % 2) * 1024:(ch % 2) * 1024 + 1024],
                                    in_=dt_["l0_ffnT"][ch * 128:(ch + 1) * 128, :])
                    ld1[ch] = (f"f{ch % 2}", c.inc(f"f{ch % 2}", i, 16))
                G_(gl)

                def vr(v, ch=ch):
                    c.w(v, ld1[ch][0], ld1[ch][1])
                    i = v.tensor_reduce(out=SM[:, 64 + ch:65 + ch],
                                        in_=SCR[:, (ch % 2) * 1024:(ch % 2) * 1024 + 1024],
                                        axis=AX, op=ALU.add, apply_absolute_value=True)
                    red[ch] = c.inc("dve", i)
                V_(vr)

            def vsum(v):
                c.w(v, "dve", red[NCH - 1])
                i = v.tensor_tensor(out=SM[:, 72:73], in0=SM[:, 64:65], in1=SM[:, 65:66], op=ALU.add)
                k = c.inc("dve", i)
                for j in range(2, NCH):
                    c.w(v, "dve", k)
                    i = v.tensor_tensor(out=SM[:, 72:73], in0=SM[:, 72:73], in1=SM[:, 64 + j:65 + j], op=ALU.add)
                    k = c.inc("dve", i)
            V_(vsum)
            RS = c.cnt["dve"]

            def td(t):
                c.w(t, "dve", RS)
                c.w(t, "pool", CONST_POOL)
                i = t.matmul(psB[0:1, 0:1], SM[:, 72:73], ONESfc[:], start=True, stop=True)
                c.inc("pe", i)
            T_(td)
            P1 = c.cnt["pe"]

            def sd(s):
                c.w(s, "pe", P1)
                i = s.activation(out=SM[0:1, 73:74], in_=psB[0:1, 0:1], func=AF.Identity, bias=0.0, scale=0.7 / (D * D))
                c.inc("act", i)
            S_(sd)
            A1 = c.cnt["act"]

            def td2(t):
                c.w(t, "act", A1)
                i = t.matmul(psB[:, 1:2], ONESfr[:], SM[0:1, 73:74], start=True, stop=True)
                c.inc("pe", i)
            T_(td2)
            P2 = c.cnt["pe"]

            def sd2(s):
                c.w(s, "pe", P2)
                i = s.activation(out=SM[:, 74:75], in_=psB[:, 1:2], func=AF.Identity, bias=0.0, scale=1.0)
                c.inc("act", i)
                i = s.activation(out=SM[:, 75:76], in_=psB[:, 1:2], func=AF.Identity, bias=0.0, scale=-1.0)
                c.inc("act", i)
            S_(sd2)
            A2 = c.cnt["act"]

            ld2 = {}
            sgn = {}
            msk = {}
            eff = {}
            for ch in range(NCH):
                def gl2(g, ch=ch):
                    if ch == 0:
                        c.w(g, "act", A2)
                        c.w(g, "dve", RS)
                    if ch >= 2:
                        c.w(g, "dve", eff[ch - 2])
                    i = g.dma_start(out=SCR[:, (ch % 2) * 1024:(ch % 2) * 1024 + 1024],
                                    in_=dt_["l0_ffnT"][ch * 128:(ch + 1) * 128, :])
                    ld2[ch] = (f"f{ch % 2}", c.inc(f"f{ch % 2}", i, 16))
                G_(gl2)

                def vt(v, ch=ch):
                    if ch == 0:
                        c.w(v, "act", A2)
                        c.w(v, "gio", CONST_GIO)
                    c.w(v, ld2[ch][0], ld2[ch][1])
                    wsrc = SCR[:, (ch % 2) * 1024:(ch % 2) * 1024 + 1024]
                    fA = Fb[:, (ch % 2) * 1024:(ch % 2) * 1024 + 1024]
                    fB = Fb[:, 2048 + (ch % 2) * 1024:2048 + (ch % 2) * 1024 + 1024]
                    if ch >= 2:
                        c.w(v, "dve", eff[ch - 2])
                    i = v.tensor_scalar(out=fA, in0=wsrc, scalar1=SM[:, 74:75], scalar2=None, op0=ALU.is_gt)
                    c.inc("dve", i)
                    i = v.tensor_scalar(out=fB, in0=wsrc, scalar1=SM[:, 75:76], scalar2=None, op0=ALU.is_lt)
                    k = c.inc("dve", i)
                    c.w(v, "dve", k)
                    i = v.tensor_tensor(out=fA, in0=fA, in1=fB, op=ALU.subtract)
                    k = c.inc("dve", i)
                    c.w(v, "dve", k)
                    i = v.tensor_scalar(out=WB[:, ch * 1024:(ch + 1) * 1024],
                                        in0=fA, scalar1=CB[:, 48:49], scalar2=None, op0=ALU.mult)
                    eff[ch] = c.inc("dve", i)
                V_(vt)
            TERN = c.cnt["dve"]
            return TERN

        def emit_l0_ffn_mm(xn_ready, TERN):
            mmf = {}
            evf = {}
            stf = {}
            act0_ffn = c.cnt["act"]
            for ofc in range(NCH):
                for tt in range(NT4):
                    evf[ofc * NT4 + tt] = act0_ffn + ofc * NT4 + tt + 1
            ffn_sq = []
            for ofc in range(NCH):
                for tt in range(NT4):
                    idx = ofc * NT4 + tt

                    def tf(t, ofc=ofc, tt=tt, idx=idx):
                        if idx == 0:
                            c.w(t, "dve", max(TERN, xn_ready))
                        if idx >= 2:
                            c.w(t, "act", evf[idx - 2])
                        bank = (idx % 2) * 512
                        for k in range(NCH):
                            i = t.matmul(psA[:, bank:bank + 512],
                                         WB[:, k * 1024 + ofc * 128:k * 1024 + (ofc + 1) * 128],
                                         Bb[:, k * TOK + tt * 512:k * TOK + (tt + 1) * 512],
                                         start=(k == 0), stop=(k == NCH - 1))
                        mmf[idx] = c.inc("pe", i)
                    T_(tf)

                    def sf(s, ofc=ofc, tt=tt, idx=idx):
                        c.w(s, "pe", mmf[idx])
                        if idx >= 4:
                            c.w(s, stf[idx - 4][0], stf[idx - 4][1])
                        i = s.activation(out=Fb[:, 4096 + (idx % 4) * 512:4096 + (idx % 4) * 512 + 512],
                                         in_=psA[:, (idx % 2) * 512:(idx % 2) * 512 + 512], func=AF.Silu,
                                         bias=CB[:, 49 + ofc:50 + ofc], scale=1.0)
                        got = c.inc("act", i)
                        assert got == evf[idx]
                    ffn_sq.append(sf)

                    def yf(sy, ofc=ofc, tt=tt, idx=idx):
                        c.w(sy, "act", evf[idx])
                        i = sy.dma_start(out=dt_["g_act"][ofc * 128:(ofc + 1) * 128, tt * 512:(tt + 1) * 512],
                                         in_=Fb[:, 4096 + (idx % 4) * 512:4096 + (idx % 4) * 512 + 512])
                        stf[idx] = (f"gs{idx % 4}", c.inc(f"gs{idx % 4}", i, 16))
                    SY_(yf)

            def sflush(s):
                for f in ffn_sq:
                    f(s)
            S_(sflush)
            return c.cnt["gs3"], c.cnt["pe"], c.cnt["act"]

        def emit_qkv(layer, xn_ready, prev_users_done):
            wname = "l0_wqkvT" if layer == 0 else "l1_wqkvT"
            wld = {}
            slot_last_use = {0: 0, 1: 0}
            qk_mm = {}
            qk_sq = {}
            qk_cp = {}
            rms_mm = {}
            rms_ln = {}
            sel_mm = {}
            stt_d = {}
            v_mm = {}
            v_ev = {}
            act0_q = c.cnt["act"]
            if layer == 0:
                for ofc_ in range(16):
                    qk_sq[ofc_] = act0_q + ofc_ * 4 + 1
                    qk_cp[ofc_] = act0_q + ofc_ * 4 + 2
                    rms_ln[ofc_] = act0_q + ofc_ * 4 + 4
                for vidx in range(32):
                    v_ev[vidx] = act0_q + 64 + vidx + 1
            else:
                for idx in range(64):
                    qk_cp[idx] = act0_q + idx + 1
                for vidx in range(32):
                    v_ev[vidx] = act0_q + 64 + vidx + 1
            qkv_sq = []
            gs_fin = {nm: c.cnt[nm] for nm in ("gs0", "gs1", "gs2", "gs3")} if layer == 0 else {}

            def vo(v):
                for nm, val in prev_users_done:
                    c.w(v, nm, val)
                for pt in range(NPT):
                    i = v.memset(Dv[:, pt * 1040:(pt + 1) * 1040]
                                 .rearrange("p (h e) -> p h e", e=65)[:, :, 64:65], 1.0)
                    c.inc("dve", i)
            V_(vo)
            VONES = c.cnt["dve"]

            for sl in range(6):
                def gw(g, sl=sl):
                    if sl == 0:
                        for nm, val in prev_users_done:
                            c.w(g, nm, val)
                    if sl >= 2:
                        c.w(g, "pe", slot_last_use[sl % 2])
                    i = g.dma_start(out=WB[:, (sl % 2) * 4096:(sl % 2) * 4096 + 4096]
                                    .rearrange("p (k o) -> p k o", k=8),
                                    in_=dt_[wname][:, sl * 512:(sl + 1) * 512]
                                    .rearrange("(k p) o -> p k o", p=128))
                    wld[sl] = (f"qw{sl % 2}", c.inc(f"qw{sl % 2}", i, 16))
                G_(gw)

                if sl < 4:
                    if layer == 0:
                        def mk_tq(ofc, sl_):
                            pp = ofc % 2

                            def tq(t):
                                if ofc == 0:
                                    c.w(t, "dve", max(xn_ready, VONES))
                                    for nm_, v_ in prev_users_done:
                                        c.w(t, nm_, v_)
                                c.w(t, wld[sl_][0], wld[sl_][1])
                                if ofc > 1:
                                    c.w(t, "act", qk_cp[ofc - 2])
                                    c.w(t, "dve", stt_d[ofc - 2])
                                ps_ = psA if pp == 0 else psB
                                for tt in range(NT4):
                                    for k in range(NCH):
                                        i = t.matmul(ps_[:, tt * 512:(tt + 1) * 512],
                                                     WB[:, (sl_ % 2) * 4096 + k * 512 + (ofc % 4) * 128:
                                                        (sl_ % 2) * 4096 + k * 512 + (ofc % 4 + 1) * 128],
                                                     Bb[:, k * TOK + tt * 512:k * TOK + (tt + 1) * 512],
                                                     start=(k == 0), stop=(k == NCH - 1))
                                qk_mm[ofc] = c.inc("pe", i)
                                slot_last_use[sl_ % 2] = qk_mm[ofc]
                            return tq

                        def mk_tail(ofc):
                            pp = ofc % 2

                            def sq0(s):
                                ps_ = psA if pp == 0 else psB
                                c.w(s, "pe", qk_mm[ofc])
                                if ofc == 0:
                                    for nm_, v_ in gs_fin.items():
                                        c.w(s, nm_, v_)
                                if ofc >= 2:
                                    c.w(s, "pe", rms_mm.get(ofc - 2, 0))
                                bcol = CB[:, ofc:ofc + 1]
                                i = s.activation(out=Fb[:, 4096 + pp * 2048:4096 + pp * 2048 + 2048],
                                                 in_=ps_[:], func=AF.Square, bias=bcol, scale=1.0)
                                assert c.inc("act", i) == qk_sq[ofc]
                                if ofc >= 2:
                                    c.w(s, "dve", stt_d[ofc - 2])
                                i = s.activation(out=Fb[:, pp * 2048:pp * 2048 + 2048],
                                                 in_=ps_[:], func=AF.Identity, bias=bcol, scale=1.0)
                                assert c.inc("act", i) == qk_cp[ofc]
                            qkv_sq.append(sq0)

                            def tr0(t):
                                psR = psA if pp == 0 else psB
                                c.w(t, "act", qk_cp[ofc])
                                for tt in range(NT4):
                                    i = t.matmul(psR[0:2, tt * 512:(tt + 1) * 512], BLK2B[:],
                                                 Fb[:, 4096 + pp * 2048 + tt * 512:4096 + pp * 2048 + (tt + 1) * 512],
                                                 start=True, stop=True)
                                rms_mm[ofc] = c.inc("pe", i)
                            T_(tr0)

                            def sr0(s):
                                psR = psA if pp == 0 else psB
                                c.w(s, "pe", rms_mm[ofc])
                                if ofc >= 2:
                                    c.w(s, "pe", sel_mm.get(ofc - 2, 0))
                                i = s.activation(out=SCR[64:66, pp * 2048:pp * 2048 + 2048],
                                                 in_=psR[0:2, :],
                                                 func=AF.Ln, bias=CB[0:2, 68:69], scale=1.0 / HD)
                                k = c.inc("act", i)
                                c.w(s, "act", k)
                                i = s.activation(out=SCR[0:2, pp * 2048:pp * 2048 + 2048],
                                                 in_=SCR[64:66, pp * 2048:pp * 2048 + 2048],
                                                 func=AF.Exp, bias=0.0, scale=-0.5)
                                assert c.inc("act", i) == rms_ln[ofc]
                            qkv_sq.append(sr0)

                            def ts0(t):
                                psR = psA if pp == 0 else psB
                                c.w(t, "act", rms_ln[ofc])
                                for tt in range(NT4):
                                    i = t.matmul(psR[:, tt * 512:(tt + 1) * 512], SEL[:],
                                                 SCR[0:2, pp * 2048 + tt * 512:pp * 2048 + (tt + 1) * 512],
                                                 start=True, stop=True)
                                sel_mm[ofc] = c.inc("pe", i)
                            T_(ts0)

                            def vs0(v):
                                psR = psA if pp == 0 else psB
                                c.w(v, "pe", sel_mm[ofc])
                                c.w(v, "act", qk_cp[ofc])
                                dst = A if ofc < 8 else C
                                dch = ofc if ofc < 8 else ofc - 8
                                pat = CB[:, 32 + ofc:33 + ofc] if ofc < 8 else CB[:, 40 + ofc - 8:41 + ofc - 8]
                                i = v.scalar_tensor_tensor(
                                    out=dst[:, dch * TOK:(dch + 1) * TOK],
                                    in0=Fb[:, pp * 2048:pp * 2048 + 2048],
                                    scalar=pat, in1=psR[:],
                                    op0=ALU.mult, op1=ALU.mult)
                                stt_d[ofc] = c.inc("dve", i)
                            V_(vs0)

                        for msub in range(4):
                            ofc = sl * 4 + msub
                            T_(mk_tq(ofc, sl))
                            if ofc >= 1:
                                mk_tail(ofc - 1)
                            if ofc == 15:
                                mk_tail(15)
                    else:
                        for msub in range(4):
                            ofc = sl * 4 + msub
                            for tt in range(NT4):
                                idx = ofc * NT4 + tt

                                def tq(t, sl=sl, msub=msub, ofc=ofc, tt=tt, idx=idx):
                                    if idx == 0:
                                        c.w(t, "dve", max(xn_ready, VONES))
                                        for nm_, v_ in prev_users_done:
                                            c.w(t, nm_, v_)
                                    c.w(t, wld[sl][0], wld[sl][1])
                                    if idx > 1:
                                        c.w(t, "act", qk_cp[idx - 2])
                                    for k in range(NCH):
                                        i = t.matmul(psA[:, (idx % 2) * 512:(idx % 2) * 512 + 512],
                                                     WB[:, (sl % 2) * 4096 + k * 512 + msub * 128:
                                                        (sl % 2) * 4096 + k * 512 + (msub + 1) * 128],
                                                     Bb[:, k * TOK + tt * 512:k * TOK + (tt + 1) * 512],
                                                     start=(k == 0), stop=(k == NCH - 1))
                                    qk_mm[idx] = c.inc("pe", i)
                                    slot_last_use[sl % 2] = qk_mm[idx]
                                T_(tq)

                                def sq1(s, ofc=ofc, tt=tt, idx=idx):
                                    c.w(s, "pe", qk_mm[idx])
                                    dst = A if ofc < 8 else C
                                    dch = ofc if ofc < 8 else ofc - 8
                                    i = s.activation(out=dst[:, dch * TOK + tt * 512:dch * TOK + (tt + 1) * 512],
                                                     in_=psA[:, (idx % 2) * 512:(idx % 2) * 512 + 512],
                                                     func=AF.Identity, bias=0.0, scale=1.0)
                                    assert c.inc("act", i) == qk_cp[idx]
                                qkv_sq.append(sq1)
                else:
                    for pt in range(NPT):
                        vidx = (sl - 4) * NPT + pt

                        def tv(t, sl=sl, pt=pt, vidx=vidx):
                            c.w(t, wld[sl][0], wld[sl][1])
                            if vidx == 0:
                                for nm_, v_ in prev_users_done:
                                    c.w(t, nm_, v_)
                                if layer == 0:
                                    c.w(t, "dve", stt_d.get(14, 0))
                            if vidx > 1:
                                c.w(t, "act", v_ev[vidx - 2])
                            for k in range(NCH):
                                i = t.matmul(psA[:, 1024 + (vidx % 2) * 512:1024 + (vidx % 2) * 512 + 512],
                                             Bb[:, k * TOK + pt * 128:k * TOK + (pt + 1) * 128],
                                             WB[:, (sl % 2) * 4096 + k * 512:(sl % 2) * 4096 + (k + 1) * 512],
                                             start=(k == 0), stop=(k == NCH - 1 and layer == 1))
                            if layer == 0:
                                c.w(t, "pool", CONST_POOL)
                                i = t.matmul(psA[:, 1024 + (vidx % 2) * 512:1024 + (vidx % 2) * 512 + 512],
                                             ONESbr[:],
                                             BROW[0:1, (sl - 4) * 512:(sl - 4) * 512 + 512],
                                             start=False, stop=True)
                            v_mm[vidx] = c.inc("pe", i)
                            slot_last_use[sl % 2] = v_mm[vidx]
                        T_(tv)

                        def sv(s, sl=sl, pt=pt, vidx=vidx):
                            c.w(s, "pe", v_mm[vidx])
                            h0 = (sl - 4) * 8
                            dstap = Dv[:, pt * 1040 + h0 * 65:pt * 1040 + h0 * 65 + 8 * 65] \
                                .rearrange("p (h e) -> p h e", e=65)[:, :, 0:64]
                            i = s.activation(out=dstap, in_=psA[:, 1024 + (vidx % 2) * 512:1024 + (vidx % 2) * 512 + 512],
                                             func=AF.Identity, bias=0.0, scale=1.0)
                            assert c.inc("act", i) == v_ev[vidx]
                        qkv_sq.append(sv)

            def qflush(s):
                for f in qkv_sq:
                    f(s)
            S_(qflush)
            return c.cnt["dve"], c.cnt["act"]

        def emit_attn(layer, deps):
            scale = L0_SCALE if layer == 0 else L1_SCALE
            act0 = c.cnt["act"]
            exp_done = {}
            rawdve = {}
            recz = {}
            for pr in range(8):
                for sqh in range(2):
                    base = act0 + pr * 68 + sqh * 34
                    for skt in range(NPT):
                        for eo in range(2):
                            exp_done[(pr, sqh, skt, eo)] = base + skt * 2 + eo + 1
                    recz[(pr, sqh, 1)] = base + 34
            sc_mm = {}
            av_mm = {}
            bcd = {}
            att_d = {}
            scalar_q = []
            pending = []

            for pr in range(8):
                for sqh in range(2):
                    for skt in range(NPT):
                        def tsc(t, pr=pr, sqh=sqh, skt=skt):
                            if pr == 0 and sqh == 0 and skt == 0:
                                for nm, val in deps:
                                    c.w(t, nm, val)
                            def _prev(eo):
                                if skt > 0:
                                    return exp_done[(pr, sqh, skt - 1, eo)]
                                if sqh == 1:
                                    return exp_done[(pr, 0, NPT - 1, eo)]
                                if pr > 0:
                                    return exp_done[(pr - 1, 1, NPT - 1, eo)]
                                return 0
                            c.w(t, "act", _prev(0))
                            for sqt in range(2):
                                sq0 = sqh * 1024 + sqt * 512
                                i = t.matmul(psA[:, sqt * 512:(sqt + 1) * 512],
                                             C[0:64, pr * TOK + skt * 128:pr * TOK + (skt + 1) * 128],
                                             A[0:64, pr * TOK + sq0:pr * TOK + sq0 + 512],
                                             start=True, stop=True, tile_position=(0, 0))
                            sc_mm[(pr, sqh, skt, 0)] = c.inc("pe", i)
                            c.w(t, "act", _prev(1))
                            for sqt in range(2):
                                sq0 = sqh * 1024 + sqt * 512
                                i = t.matmul(psA[:, 1024 + sqt * 512:1024 + (sqt + 1) * 512],
                                             C[64:128, pr * TOK + skt * 128:pr * TOK + (skt + 1) * 128],
                                             A[64:128, pr * TOK + sq0:pr * TOK + sq0 + 512],
                                             start=True, stop=True, tile_position=(64, 0))
                            sc_mm[(pr, sqh, skt, 1)] = c.inc("pe", i)
                            if skt > 0:
                                ps_ = skt - 1
                                w = exp_done[(pr, sqh, ps_, 1)]
                                if ps_ == 0:
                                    if sqh == 1:
                                        w = max(w, recz[(pr, 0, 1)])
                                    elif pr > 0:
                                        w = max(w, recz[(pr - 1, 1, 1)])
                                c.w(t, "act", w)
                                if ps_ == 0:
                                    if sqh == 1 and (pr, 0) in att_d:
                                        c.w(t, "dve", att_d[(pr, 0)])
                                    elif sqh == 0 and pr > 0 and (pr - 1, 1) in att_d:
                                        c.w(t, "dve", att_d[(pr - 1, 1)])
                                for eo in range(2):
                                    h = pr * 2 + eo
                                    for sqt in range(2):
                                        i = t.matmul(psB[0:65, eo * 1024 + sqt * 512:eo * 1024 + (sqt + 1) * 512],
                                                     Dv[:, ps_ * 1040 + h * 65:ps_ * 1040 + h * 65 + 65],
                                                     Eb[:, (ps_ % 2) * 2048 + eo * 1024 + sqt * 512:
                                                        (ps_ % 2) * 2048 + eo * 1024 + (sqt + 1) * 512],
                                                     start=(ps_ == 0), stop=False)
                                av_mm[(pr, sqh, ps_)] = c.inc("pe", i)
                        T_(tsc)
                        if skt == 0 and pending:
                            T_(pending[0])
                            V_(pending[1])
                            pending = []

                        def sex(s, pr=pr, sqh=sqh, skt=skt):
                            for eo in range(2):
                                w = sc_mm[(pr, sqh, skt, eo)]
                                if eo == 0 and skt >= 2:
                                    w = max(w, av_mm[(pr, sqh, skt - 2)])
                                c.w(s, "pe", w)
                                i = s.activation(out=Eb[:, (skt % 2) * 2048 + eo * 1024:(skt % 2) * 2048 + eo * 1024 + 1024],
                                                 in_=psA[:, eo * 1024:eo * 1024 + 1024],
                                                 func=AF.Exp, bias=0.0, scale=scale)
                                got = c.inc("act", i)
                                assert got == exp_done[(pr, sqh, skt, eo)]
                        scalar_q.append(sex)

                    def tav(t, pr=pr, sqh=sqh):
                        ps_ = NPT - 1
                        c.w(t, "act", exp_done[(pr, sqh, ps_, 1)])
                        for eo in range(2):
                            h = pr * 2 + eo
                            for sqt in range(2):
                                i = t.matmul(psB[0:65, eo * 1024 + sqt * 512:eo * 1024 + (sqt + 1) * 512],
                                             Dv[:, ps_ * 1040 + h * 65:ps_ * 1040 + h * 65 + 65],
                                             Eb[:, (ps_ % 2) * 2048 + eo * 1024 + sqt * 512:
                                                (ps_ % 2) * 2048 + eo * 1024 + (sqt + 1) * 512],
                                             start=False, stop=True)
                        av_mm[(pr, sqh, ps_)] = c.inc("pe", i)
                    T_(tav)

                    def vraw(v, pr=pr, sqh=sqh):
                        c.w(v, "pe", av_mm[(pr, sqh, NPT - 1)])
                        if (pr, sqh) != (0, 0):
                            pk = (pr, 0) if sqh == 1 else (pr - 1, 1)
                            c.w(v, "dve", att_d[pk])
                        i = v.tensor_copy(out=SCR[0:64, 0:2048], in_=psB[0:64, :])
                        rawdve[(pr, sqh)] = c.inc("dve", i)
                    V_(vraw)

                    def sraw(s, pr=pr, sqh=sqh):
                        c.w(s, "pe", av_mm[(pr, sqh, NPT - 1)])
                        i = s.activation(out=SCR[32:33, 2048:4096], in_=psB[64:65, 0:2048],
                                         func=AF.Ln, bias=CB[0:1, 69:70], scale=1.0)
                        k = c.inc("act", i)
                        c.w(s, "act", k)
                        i = s.activation(out=SCR[0:1, 2048:4096], in_=SCR[32:33, 2048:4096],
                                         func=AF.Exp, bias=0.0, scale=-1.0)
                        got = c.inc("act", i)
                        assert got == recz[(pr, sqh, 1)]
                    scalar_q.append(sraw)

                    def tbc(t, pr=pr, sqh=sqh):
                        c.w(t, "act", recz[(pr, sqh, 1)])
                        c.w(t, "dve", rawdve[(pr, sqh)])
                        for eo in range(2):
                            for sqt in range(2):
                                i = t.matmul(psB[0:64, eo * 1024 + sqt * 512:eo * 1024 + (sqt + 1) * 512],
                                             ONESfr[0:1, 0:64],
                                             SCR[0:1, 2048 + eo * 1024 + sqt * 512:2048 + eo * 1024 + (sqt + 1) * 512],
                                             start=True, stop=True)
                        bcd[(pr, sqh)] = c.inc("pe", i)

                    def vn(v, pr=pr, sqh=sqh):
                        c.w(v, "pe", bcd[(pr, sqh)])
                        for eo in range(2):
                            i = v.tensor_tensor(out=A[64 * eo:64 * eo + 64,
                                                      pr * TOK + sqh * 1024:pr * TOK + sqh * 1024 + 1024],
                                                in0=SCR[0:64, eo * 1024:eo * 1024 + 1024],
                                                in1=psB[0:64, eo * 1024:eo * 1024 + 1024], op=ALU.mult)
                            att_d[(pr, sqh)] = c.inc("dve", i)
                    pending = [tbc, vn]
            T_(pending[0])
            V_(pending[1])

            def sall(s):
                for f in scalar_q:
                    f(s)
            S_(sall)
            return c.cnt["dve"]

        def emit_proj_residual(layer, attn_ready, extra_deps=()):
            wname = "l0_wprojT" if layer == 0 else "l1_woT"
            x_src = dt_["xT"] if layer == 0 else dt_["x_mid"]
            out_d = dt_["x_mid"] if layer == 0 else dt_["outT"]
            out_f32 = (layer == 1)
            wld = {}
            gld = {}
            xld = {}
            mmp = {}
            sgd = {}
            res = {}
            std = {}

            def gw(g):
                c.w(g, "dve", attn_ready)
                for nm_, v_ in extra_deps:
                    c.w(g, nm_, v_)
                for sl in range(2):
                    i = g.dma_start(out=WB[:, sl * 4096:(sl + 1) * 4096].rearrange("p (k o) -> p k o", k=8),
                                    in_=dt_[wname][:, sl * 512:(sl + 1) * 512].rearrange("(k p) o -> p k o", p=128))
                    wld[sl] = (f"qw{sl % 2}", c.inc(f"qw{sl % 2}", i, 16))
            G_(gw)

            for ofc in range(NCH):
                def gx(g, ofc=ofc):
                    if ofc < 2:
                        for nm_ in ("gs0", "gs1", "gs2", "gs3", "g10", "g11", "g12", "g13"):
                            if nm_ in c.cnt:
                                c.w(g, nm_)
                    if ofc >= 2:
                        c.w(g, "dve", max(sgd[ofc - 2], res[ofc - 2]))
                    i = g.dma_start(out=Fb[:, (ofc % 2) * 2048:(ofc % 2) * 2048 + 2048],
                                    in_=dt_["g_act"][ofc * 128:(ofc + 1) * 128, :])
                    c.inc(f"pg{ofc % 2}", i, 16)
                    i = g.dma_start(out=Fb[:, 4096 + (ofc % 2) * 2048:4096 + (ofc % 2) * 2048 + 2048],
                                    in_=x_src[ofc * 128:(ofc + 1) * 128, :])
                    xld[ofc] = (f"pg{ofc % 2}", c.inc(f"pg{ofc % 2}", i, 16))
                G_(gx)

                def tp(t, ofc=ofc):
                    sl, msub = ofc // 4, ofc % 4
                    c.w(t, wld[sl][0], wld[sl][1])
                    if ofc >= 2:
                        c.w(t, "dve", sgd[ofc - 2])
                    ps = psA if ofc % 2 == 0 else psB
                    for tt in range(NT4):
                        for k in range(NCH):
                            i = t.matmul(ps[:, tt * 512:(tt + 1) * 512],
                                         WB[:, sl * 4096 + k * 512 + msub * 128:sl * 4096 + k * 512 + (msub + 1) * 128],
                                         A[:, k * TOK + tt * 512:k * TOK + (tt + 1) * 512],
                                         start=(k == 0), stop=(k == NCH - 1))
                    mmp[ofc] = c.inc("pe", i)
                T_(tp)

                def vp(v, ofc=ofc):
                    ps = psA if ofc % 2 == 0 else psB
                    c.w(v, "pe", mmp[ofc])
                    c.w(v, xld[ofc][0], xld[ofc][1])
                    bias = CB[:, 24 + ofc:25 + ofc] if layer == 0 else 0.0
                    i = v.scalar_tensor_tensor(out=Eb[:, (ofc % 2) * 2048:(ofc % 2) * 2048 + 2048],
                                               in0=ps[:], scalar=bias,
                                               in1=Fb[:, (ofc % 2) * 2048:(ofc % 2) * 2048 + 2048],
                                               op0=ALU.add, op1=ALU.mult)
                    sgd[ofc] = c.inc("dve", i)
                    if ofc >= 2:
                        c.w(v, std[ofc - 2][0], std[ofc - 2][1])
                    c.w(v, "dve", sgd[ofc])
                    i = v.tensor_tensor(out=SCR[:, (ofc % 2) * 2048:(ofc % 2) * 2048 + 2048],
                                        in0=Fb[:, 4096 + (ofc % 2) * 2048:4096 + (ofc % 2) * 2048 + 2048],
                                        in1=Eb[:, (ofc % 2) * 2048:(ofc % 2) * 2048 + 2048], op=ALU.add)
                    res[ofc] = c.inc("dve", i)
                V_(vp)

                def gs(g, ofc=ofc):
                    c.w(g, "dve", res[ofc])
                    i = g.dma_start(out=out_d[ofc * 128:(ofc + 1) * 128, :],
                                    in_=SCR[:, (ofc % 2) * 2048:(ofc % 2) * 2048 + 2048])
                    std[ofc] = (f"xs{ofc % 2}", c.inc(f"xs{ofc % 2}", i, 16))
                    c.inc("gst", i, 0) if False else None
                G_(gs)
            return c.cnt["gst"]

        def emit_qpk(xn_ready, prev_done):
            def gw(g):
                for nm, v in prev_done:
                    c.w(g, nm, v)
                i = g.dma_start(out=WB[:, 0:4096].rearrange("p (k o) -> p k o", k=8),
                                in_=dt_["wq_p"].rearrange("(k p) o -> p k o", p=128))
                c.inc("gio", i, 16)
            G_(gw)
            WLD = c.cnt["gio"]
            mmq = {}
            evq = {}
            act0_p = c.cnt["act"]
            for i_ in range(16):
                evq[i_] = act0_p + i_ + 1
            qpk_sq = []
            for hch in range(4):
                for tt in range(NT4):
                    idx = hch * NT4 + tt

                    def tq(t, hch=hch, tt=tt, idx=idx):
                        if idx == 0:
                            c.w(t, "gio", WLD)
                            c.w(t, "dve", xn_ready)
                        if idx >= 2:
                            c.w(t, "act", evq[idx - 2])
                        bank = (idx % 2) * 512
                        for k in range(NCH):
                            i = t.matmul(psA[:, bank:bank + 512],
                                         WB[:, k * 512 + hch * 128:k * 512 + (hch + 1) * 128],
                                         Bb[:, k * TOK + tt * 512:k * TOK + (tt + 1) * 512],
                                         start=(k == 0), stop=(k == NCH - 1))
                        mmq[idx] = c.inc("pe", i)
                    T_(tq)

                    def sq(s, hch=hch, tt=tt, idx=idx):
                        c.w(s, "pe", mmq[idx])
                        i = s.activation(out=Eb[:, hch * TOK + tt * 512:hch * TOK + (tt + 1) * 512],
                                         in_=psA[:, (idx % 2) * 512:(idx % 2) * 512 + 512],
                                         func=AF.Identity, bias=0.0, scale=1.0)
                        assert c.inc("act", i) == evq[idx]
                    qpk_sq.append(sq)

            def pflush(s):
                for f in qpk_sq:
                    f(s)
            S_(pflush)
            return c.cnt["act"]

        def emit_lists(qpk_ready):
            simmm = {}
            simev = {}
            lists_dve = {}
            act0_l = c.cnt["act"]
            E8F = {pt_: act0_l + pt_ * 5 + 4 for pt_ in range(NPT)}
            LNZF = {pt_: act0_l + pt_ * 5 + 5 for pt_ in range(NPT)}
            lists_sq = []
            for pt in range(NPT):
                V8 = lambda pt, h: SCR[:, 1024 + pt * 40 + h * 8:1024 + pt * 40 + h * 8 + 8]
                NEGMC = lambda pt, h: SCR[:, 1664 + pt * 4 + h:1664 + pt * 4 + h + 1]
                E8C = lambda pt, h: SM[:, 756 + pt * 32 + h * 8:756 + pt * 32 + h * 8 + 8]
                ZC = lambda pt: SCR[:, 1856 + pt * 4:1856 + pt * 4 + 4]
                LNZC = lambda pt: SCR[:, 1728 + pt * 4:1728 + pt * 4 + 4]

                def tsim(t, pt=pt):
                    if pt == 0:
                        c.w(t, "act", qpk_ready)
                    if pt > 0:
                        c.w(t, "dve", simev[pt - 1])
                    for h in range(4):
                        for p in range(2):
                            i = t.matmul(psB[:, p * 512 + h * 64:p * 512 + (h + 1) * 64],
                                         Eb[64 * p:64 * p + 64, h * TOK + pt * 128:h * TOK + (pt + 1) * 128],
                                         KEYSP[64 * p:64 * p + 64, h * 64:(h + 1) * 64],
                                         start=True, stop=True, tile_position=(64 * p, 0))
                    simmm[pt] = c.inc("pe", i)
                T_(tsim)

                def vtop(v, pt=pt, V8=V8, NEGMC=NEGMC):
                    c.w(v, "pe", simmm[pt])
                    if pt > 0:
                        c.w(v, "dve", lists_dve[pt - 1])
                    i = v.tensor_copy(out=SCR[:, 0:1024], in_=psB[:, 0:1024])
                    simev[pt] = c.inc("dve", i)
                    c.w(v, "dve", simev[pt])
                    for ph in range(8):
                        h_, p_ = ph // 2, ph % 2
                        i = v.max(out=SM[:, 128 + ph * 8:128 + ph * 8 + 8],
                                  in_=SCR[:, p_ * 512 + h_ * 64:p_ * 512 + (h_ + 1) * 64])
                        c.inc("dve", i)
                    k = c.cnt["dve"]
                    for h in range(4):
                        c.w(v, "dve", k)
                        i = v.tensor_tensor(out=SM[:, 192 + h * 64:192 + h * 64 + 64]
                                            .rearrange("p (a b) -> p a b", a=8),
                                            in0=SM[:, 128 + 2 * h * 8:128 + 2 * h * 8 + 8]
                                            .unsqueeze(2).to_broadcast([128, 8, 8]),
                                            in1=SM[:, 128 + (2 * h + 1) * 8:128 + (2 * h + 1) * 8 + 8]
                                            .unsqueeze(1).to_broadcast([128, 8, 8]),
                                            op=ALU.add)
                        c.inc("dve", i)
                    k = c.cnt["dve"]
                    for h in range(4):
                        c.w(v, "dve", k)
                        i = v.max(out=V8(pt, h), in_=SM[:, 192 + h * 64:192 + h * 64 + 64])
                        k2 = c.inc("dve", i)
                        c.w(v, "dve", k2)
                        i = v.match_replace(out=SM[:, 480:544], in_to_replace=V8(pt, h),
                                            in_values=SM[:, 192 + h * 64:192 + h * 64 + 64], imm_value=-1e30)
                        k2 = c.inc("dve", i)
                        c.w(v, "dve", k2)
                        i = v.reduce_max(out=SM[:, 544 + h:545 + h], in_=SM[:, 480:544], axis=AX)
                        k2 = c.inc("dve", i)
                        c.w(v, "dve", k2)
                        i = v.match_replace(out=SM[:, 480:544], in_to_replace=SM[:, 128 + 2 * h * 8:128 + 2 * h * 8 + 8],
                                            in_values=SCR[:, h * 64:(h + 1) * 64], imm_value=-1e30)
                        k2 = c.inc("dve", i)
                        c.w(v, "dve", k2)
                        i = v.reduce_max(out=SM[:, 736 + h:737 + h], in_=SM[:, 480:544], axis=AX)
                        k2 = c.inc("dve", i)
                        c.w(v, "dve", k2)
                        i = v.match_replace(out=SM[:, 480:544], in_to_replace=SM[:, 128 + (2 * h + 1) * 8:128 + (2 * h + 1) * 8 + 8],
                                            in_values=SCR[:, 512 + h * 64:512 + (h + 1) * 64], imm_value=-1e30)
                        k2 = c.inc("dve", i)
                        c.w(v, "dve", k2)
                        i = v.reduce_max(out=SM[:, 740 + h:741 + h], in_=SM[:, 480:544], axis=AX)
                        k2 = c.inc("dve", i)
                        c.w(v, "dve", k2)
                    xy8 = SM[:, 128:192].rearrange("p (ph e) -> p ph e", e=8)
                    y1v = xy8[:, 1::2, 0:1]
                    x1v = xy8[:, 0::2, 0:1]
                    v8blk = SCR[:, 1024 + pt * 40:1024 + pt * 40 + 32].rearrange("p (h e) -> p h e", e=8)
                    c.w(v, "dve", c.cnt["dve"])
                    i = v.tensor_tensor(out=SM[:, 744:748].unsqueeze(2), in0=SM[:, 736:740].unsqueeze(2),
                                        in1=y1v, op=ALU.add)
                    c.inc("dve", i)
                    i = v.tensor_tensor(out=SM[:, 748:752].unsqueeze(2), in0=SM[:, 740:744].unsqueeze(2),
                                        in1=x1v, op=ALU.add)
                    k2 = c.inc("dve", i)
                    c.w(v, "dve", k2)
                    i = v.tensor_tensor(out=SM[:, 752:756], in0=SM[:, 744:748],
                                        in1=SM[:, 748:752], op=ALU.max)
                    k2 = c.inc("dve", i)
                    c.w(v, "dve", k2)
                    i = v.tensor_tensor(out=SM[:, 544:548], in0=SM[:, 544:548],
                                        in1=SM[:, 752:756], op=ALU.max)
                    k2 = c.inc("dve", i)
                    c.w(v, "dve", k2)
                    i = v.tensor_tensor(out=SM[:, 548:552].unsqueeze(2),
                                        in0=v8blk[:, :, 7:8],
                                        in1=SM[:, 544:548].unsqueeze(2), op=ALU.add)
                    k2 = c.inc("dve", i)
                    c.w(v, "dve", k2)
                    i = v.tensor_scalar(out=SM[:, 560 + pt * 4:564 + pt * 4],
                                        in0=SM[:, 548:552], scalar1=0.5, scalar2=None, op0=ALU.mult)
                    c.inc("dve", i)
                    i = v.tensor_scalar(out=SCR[:, 1664 + pt * 4:1668 + pt * 4].unsqueeze(2),
                                        in0=v8blk[:, :, 0:1],
                                        scalar1=-1.0, scalar2=None, op0=ALU.mult)
                    c.inc("dve", i)
                    lists_dve[pt] = c.cnt["dve"]
                V_(vtop)
                NEGM = c.cnt["dve"]

                def sexp(s, pt=pt, NEGM=NEGM, V8=V8, NEGMC=NEGMC, E8C=E8C, ZC=ZC, LNZC=LNZC):
                    c.w(s, "dve", NEGM)
                    for h in range(4):
                        i = s.activation(out=E8C(pt, h), in_=V8(pt, h),
                                         func=AF.Exp, bias=NEGMC(pt, h), scale=1.0)
                        c.inc("act", i)
                lists_sq.append(sexp)

                def vsum(v, pt=pt, E8C=E8C, ZC=ZC):
                    c.w(v, "act", E8F[pt])
                    for h in range(4):
                        i = v.reduce_sum(out=ZC(pt)[:, h:h + 1], in_=E8C(pt, h), axis=AX)
                        c.inc("dve", i)
                V_(vsum)
                ZS = c.cnt["dve"]

                def sln(s, pt=pt, ZS=ZS, ZC=ZC, LNZC=LNZC):
                    c.w(s, "dve", ZS)
                    i = s.activation(out=LNZC(pt), in_=ZC(pt), func=AF.Ln, bias=0.0, scale=1.0)
                    assert c.inc("act", i) == LNZF[pt]
                lists_sq.append(sln)

                def vnegc(v, pt=pt, NEGMC=NEGMC, LNZC=LNZC):
                    c.w(v, "act", LNZF[pt])
                    i = v.tensor_tensor(out=SM[:, 624 + pt * 4:628 + pt * 4],
                                        in0=SCR[:, 1664 + pt * 4:1668 + pt * 4],
                                        in1=SCR[:, 1728 + pt * 4:1732 + pt * 4],
                                        op=ALU.subtract)
                    c.inc("dve", i)
                    lists_dve[pt] = c.cnt["dve"]
                V_(vnegc)

            def lflush(s):
                for f in lists_sq:
                    f(s)
            S_(lflush)
            return c.cnt["dve"]

        def emit_wfull(xn_ready):
            def gk(g):
                c.w(g, "dve", xn_ready)
                i = g.dma_start(out=C[:].rearrange("p (h e) -> p h e", h=4),
                                in_=dt_["ktp"].rearrange("(h p) e -> p h e", p=128))
                c.inc("gio", i, 16)
            G_(gk)
            KTP_LD = c.cnt["gio"]
            slot_use = {0: 0, 1: 0}
            wld = {}
            mmw = {}
            evw = {}
            stw = {}
            act0_w = c.cnt["act"]
            for i_ in range(NCH * NPT):
                evw[i_] = act0_w + i_ + 1
            wf_sq = []
            for et in range(NCH):
                def gw(g, et=et):
                    if et >= 2:
                        c.w(g, "pe", slot_use[et % 2])
                    i = g.dma_start(out=WB[:, (et % 2) * 4096:(et % 2) * 4096 + 4096]
                                    .rearrange("p (k o) -> p k o", k=8),
                                    in_=dt_["down_T"][:, et * 512:(et + 1) * 512]
                                    .rearrange("(k p) o -> p k o", p=128))
                    wld[et] = (f"dw{et % 2}", c.inc(f"dw{et % 2}", i, 16))
                G_(gw)
                for pt in range(NPT):
                    idx = et * NPT + pt

                    def tw(t, et=et, pt=pt, idx=idx):
                        if idx == 0:
                            c.w(t, "dve", xn_ready)
                            c.w(t, "act", c.cnt["act"])
                        c.w(t, wld[et][0], wld[et][1])
                        if idx >= 2:
                            c.w(t, "act", evw[idx - 2])
                        bank = (idx % 2) * 512
                        for k in range(NCH):
                            i = t.matmul(psA[:, bank:bank + 512],
                                         Bb[:, k * TOK + pt * 128:k * TOK + (pt + 1) * 128],
                                         WB[:, (et % 2) * 4096 + k * 512:(et % 2) * 4096 + (k + 1) * 512],
                                         start=(k == 0), stop=(k == NCH - 1))
                        mmw[idx] = c.inc("pe", i)
                        slot_use[et % 2] = mmw[idx]
                    T_(tw)

                    def sw_(s, et=et, pt=pt, idx=idx):
                        c.w(s, "pe", mmw[idx])
                        if idx >= 4:
                            c.w(s, stw[idx - 4][0], stw[idx - 4][1])
                        i = s.activation(out=Fb[:, (idx % 4) * 512:(idx % 4) * 512 + 512],
                                         in_=psA[:, (idx % 2) * 512:(idx % 2) * 512 + 512],
                                         func=AF.Silu, bias=0.0, scale=1.0)
                        assert c.inc("act", i) == evw[idx]
                    wf_sq.append(sw_)

                    def yw(sy, et=et, pt=pt, idx=idx):
                        c.w(sy, "act", evw[idx])
                        i = sy.dma_start(out=dt_["sw_dram"][pt * 128:(pt + 1) * 128, et * 512:(et + 1) * 512],
                                         in_=Fb[:, (idx % 4) * 512:(idx % 4) * 512 + 512])
                        stw[idx] = (f"gs{idx % 4}", c.inc(f"gs{idx % 4}", i, 16))
                    SY_(yw)
            def wflush(s):
                for f in wf_sq:
                    f(s)
            S_(wflush)
            st_deps = [(f"gs{j}", c.cnt.get(f"gs{j}", 0)) for j in range(4)]
            return st_deps, KTP_LD, c.cnt["act"]

        def emit_grid(wfull_st, ktp_ld, wfull_act, lists_done):
            swl = {}
            gmm = {}
            gex = {}
            gstt = {}
            gacc = {}
            gtree = {}
            gmul = {}
            gout = {}
            act0_g = c.cnt["act"]
            for i_ in range(NPT * 8):
                gex[i_] = act0_g + i_ + 1
            grid_sq = []
            for pt in range(NPT):
                def gsw(g, pt=pt):
                    if pt == 0:
                        for nm_, v_ in wfull_st:
                            c.w(g, nm_, v_)
                        c.w(g, "gio", ktp_ld)
                    if pt >= 2:
                        c.w(g, "dve", gmul.get(pt - 2, 0))
                    i = g.dma_start(out=Dv[:, (pt % 2) * 4096:(pt % 2) * 4096 + 4096],
                                    in_=dt_["sw_dram"][pt * 128:(pt + 1) * 128, :])
                    swl[pt] = (f"sl{pt % 2}", c.inc(f"sl{pt % 2}", i, 16))
                G_(gsw)
                for h in range(4):
                    for hb in range(2):
                        sidx = ((pt * 4 + h) * 2 + hb)

                        def tg(t, pt=pt, h=h, hb=hb, sidx=sidx):
                            psS = psA if sidx % 2 == 0 else psB
                            if sidx == 0:
                                c.w(t, "gio", ktp_ld)
                                c.w(t, "act", wfull_act)
                                c.w(t, "dve", lists_done)
                            if sidx > 1:
                                c.w(t, "dve", gstt.get(sidx - 2, 0))
                            for et in range(4):
                                i = t.matmul(psS[:, et * 512:(et + 1) * 512],
                                             Eb[:, h * TOK + pt * 128:h * TOK + (pt + 1) * 128],
                                             C[:, h * 4096 + hb * 2048 + et * 512:h * 4096 + hb * 2048 + (et + 1) * 512],
                                             start=True, stop=True)
                            gmm[sidx] = c.inc("pe", i)
                        T_(tg)

                        def sg(s, pt=pt, h=h, hb=hb, sidx=sidx):
                            psS = psA if sidx % 2 == 0 else psB
                            c.w(s, "pe", gmm[sidx])
                            if sidx < 2:
                                for nm_, v_ in wfull_st:
                                    c.w(s, nm_, v_)
                            if sidx >= 2:
                                c.w(s, "dve", gstt.get(sidx - 1, 0))
                            i = s.activation(out=Fb[:, (sidx % 2) * 2048:(sidx % 2) * 2048 + 2048],
                                             in_=psS[:], func=AF.Exp,
                                             bias=SM[:, 624 + pt * 4 + h:625 + pt * 4 + h], scale=1.0)
                            assert c.inc("act", i) == gex[sidx]
                        grid_sq.append(sg)

                        def vg(v, pt=pt, h=h, hb=hb, sidx=sidx):
                            psS = psA if sidx % 2 == 0 else psB
                            c.w(v, "act", gex[sidx])
                            PBs = [Dv[:, 8192 + hb * 2048:8192 + hb * 2048 + 2048],
                                   Dv[:, 12288 + hb * 2048:12288 + hb * 2048 + 2048],
                                   WB[:, hb * 2048:hb * 2048 + 2048],
                                   WB[:, 4096 + hb * 2048:4096 + hb * 2048 + 2048]]
                            if pt > 0:
                                if h <= 1:
                                    c.w(v, "dve", gtree.get((pt - 1, hb, "t1"), 0))
                                elif (pt - 1, hb, "t2") in gtree:
                                    c.w(v, "pool", gtree[(pt - 1, hb, "t2")])
                            i = v.scalar_tensor_tensor(out=PBs[h], in0=psS[:],
                                                       scalar=SM[:, 560 + pt * 4 + h:561 + pt * 4 + h],
                                                       in1=Fb[:, (sidx % 2) * 2048:(sidx % 2) * 2048 + 2048],
                                                       op0=ALU.is_ge, op1=ALU.mult)
                            gstt[sidx] = c.inc("dve", i)
                        V_(vg)

                for hb in range(2):
                    def vT1(v, pt=pt, hb=hb):
                        c.w(v, "dve", gstt[(pt * 4 + 1) * 2 + hb])
                        c.w(v, "dve", gstt[(pt * 4 + 0) * 2 + hb])
                        i = v.tensor_tensor(out=Dv[:, 8192 + hb * 2048:8192 + hb * 2048 + 2048],
                                            in0=Dv[:, 8192 + hb * 2048:8192 + hb * 2048 + 2048],
                                            in1=Dv[:, 12288 + hb * 2048:12288 + hb * 2048 + 2048],
                                            op=ALU.add)
                        gtree[(pt, hb, "t1a")] = c.inc("dve", i)
                    V_(vT1)

                    def pT2(g, pt=pt, hb=hb):
                        c.w(g, "dve", gstt[(pt * 4 + 3) * 2 + hb])
                        c.w(g, "dve", gstt[(pt * 4 + 2) * 2 + hb])
                        i = g.tensor_tensor(out=WB[:, hb * 2048:hb * 2048 + 2048],
                                            in0=WB[:, hb * 2048:hb * 2048 + 2048],
                                            in1=WB[:, 4096 + hb * 2048:4096 + hb * 2048 + 2048],
                                            op=ALU.add)
                        gtree[(pt, hb, "t2")] = c.inc("pool", i)
                    G_(pT2)

                    def vT3(v, pt=pt, hb=hb):
                        c.w(v, "pool", gtree[(pt, hb, "t2")])
                        c.w(v, "dve", gtree[(pt, hb, "t1a")])
                        i = v.tensor_tensor(out=Dv[:, 8192 + hb * 2048:8192 + hb * 2048 + 2048],
                                            in0=Dv[:, 8192 + hb * 2048:8192 + hb * 2048 + 2048],
                                            in1=WB[:, hb * 2048:hb * 2048 + 2048],
                                            op=ALU.add)
                        gtree[(pt, hb, "t1")] = c.inc("dve", i)
                        c.w(v, swl[pt][0], swl[pt][1])
                        c.w(v, "dve", gtree[(pt, hb, "t1")])
                        if pt >= 1:
                            c.w(v, "gG", gout[pt - 1])
                        i = v.tensor_tensor(out=Fb[:, 4096 + hb * 2048:4096 + hb * 2048 + 2048],
                                            in0=Dv[:, 8192 + hb * 2048:8192 + hb * 2048 + 2048],
                                            in1=Dv[:, (pt % 2) * 4096 + hb * 2048:(pt % 2) * 4096 + hb * 2048 + 2048],
                                            op=ALU.mult)
                        gmul[pt] = c.inc("dve", i)
                    V_(vT3)

                def yG(sy, pt=pt):
                    c.w(sy, "dve", gmul[pt])
                    i = sy.dma_start(out=dt_["g_dram"][pt * 128:(pt + 1) * 128, :], in_=Fb[:, 4096:8192])
                    gout[pt] = c.inc("gG", i, 16)
                SY_(yG)

            def gflush(s):
                for f in grid_sq:
                    f(s)
            S_(gflush)
            return c.cnt["gG"]

        def emit_swiglu(xn_ready, grid_deps):
            act0_s = c.cnt["act"]
            evg = {}
            evu = {}
            evm = {}
            for p2 in range(2):
                base = act0_s + p2 * 144
                for dsl in range(8):
                    for msub in range(4):
                        for nh in range(2):
                            k_ = dsl * 8 + msub * 2 + nh
                            evg[(p2, k_)] = base + k_ * 2 + 1
                            evu[(p2, k_)] = base + k_ * 2 + 2
                for hf in range(2):
                    for dc in range(8):
                        evm[(p2, hf, dc)] = base + 128 + hf * 8 + dc + 1
            sw_sq = []
            wldg = {}
            wldu = {}
            wldd = {}
            mmg = {}
            mmu = {}
            hml = {}
            mmm = {}
            for p2 in range(2):
                for dsl in range(8):
                    key = p2 * 8 + dsl

                    def gw(g, p2=p2, dsl=dsl, key=key):
                        if key == 0:
                            for nm_, v_ in grid_deps:
                                c.w(g, nm_, v_)
                        par = dsl % 2
                        if key >= 2:
                            c.w(g, "pe", mmu.get(key - 2, 0))
                        i = g.dma_start(out=WB[:, par * 4096:par * 4096 + 4096]
                                        .rearrange("p (k o) -> p k o", k=8),
                                        in_=dt_["wg"][:, dsl * 512:(dsl + 1) * 512]
                                        .rearrange("(k p) o -> p k o", p=128))
                        wldg[key] = (f"sg{par}", c.inc(f"sg{par}", i, 16))
                        i = g.dma_start(out=Eb[:, par * 4096:par * 4096 + 4096]
                                        .rearrange("p (k o) -> p k o", k=8),
                                        in_=dt_["wu"][:, dsl * 512:(dsl + 1) * 512]
                                        .rearrange("(k p) o -> p k o", p=128))
                        wldu[key] = (f"su{par}", c.inc(f"su{par}", i, 16))
                    G_(gw)

                    for msub in range(4):
                        for nh in range(2):
                            k_ = dsl * 8 + msub * 2 + nh
                            par = dsl % 2
                            bcol = dsl * 4 + msub
                            tt = p2 * 2 + nh

                            def tg_(t, p2=p2, dsl=dsl, key=key, msub=msub, nh=nh, k_=k_, par=par, tt=tt):
                                if k_ == 0:
                                    if p2 == 0:
                                        c.w(t, "dve", xn_ready)
                                    else:
                                        c.w(t, "act", evm[(p2 - 1, 1, 7)])
                                c.w(t, wldg[key][0], wldg[key][1])
                                if k_ >= 2:
                                    c.w(t, "act", evg.get((p2, k_ - 2), 0))
                                bank = (k_ % 2) * 512
                                for k in range(NCH):
                                    i = t.matmul(psA[:, bank:bank + 512],
                                                 WB[:, par * 4096 + k * 512 + msub * 128:par * 4096 + k * 512 + (msub + 1) * 128],
                                                 Bb[:, k * TOK + tt * 512:k * TOK + (tt + 1) * 512],
                                                 start=(k == 0), stop=(k == NCH - 1))
                                mmg[(p2, k_)] = c.inc("pe", i)
                                c.w(t, wldu[key][0], wldu[key][1])
                                if k_ >= 2:
                                    c.w(t, "act", evu.get((p2, k_ - 2), 0))
                                for k in range(NCH):
                                    i = t.matmul(psA[:, 1024 + bank:1024 + bank + 512],
                                                 Eb[:, par * 4096 + k * 512 + msub * 128:par * 4096 + k * 512 + (msub + 1) * 128],
                                                 Bb[:, k * TOK + tt * 512:k * TOK + (tt + 1) * 512],
                                                 start=(k == 0), stop=(k == NCH - 1))
                                mmu[key] = c.inc("pe", i)
                                mmg[(p2, k_, "u")] = mmu[key]
                            T_(tg_)

                            def sv_(s, p2=p2, k_=k_, bcol=bcol):
                                bank = (k_ % 2) * 512
                                c.w(s, "pe", mmg[(p2, k_, "u")])
                                if k_ >= 2:
                                    c.w(s, "dve", hml.get((p2, k_ - 2), 0))
                                i = s.activation(out=Fb[:, (k_ % 2) * 512:(k_ % 2) * 512 + 512],
                                                 in_=psA[:, bank:bank + 512], func=AF.Silu,
                                                 bias=BGU[:, bcol:bcol + 1], scale=1.0)
                                assert c.inc("act", i) == evg[(p2, k_)]
                                i = s.activation(out=Fb[:, 1024 + (k_ % 2) * 512:1024 + (k_ % 2) * 512 + 512],
                                                 in_=psA[:, 1024 + bank:1024 + bank + 512], func=AF.Identity,
                                                 bias=BGU[:, 32 + bcol:33 + bcol], scale=1.0)
                                assert c.inc("act", i) == evu[(p2, k_)]
                            sw_sq.append(sv_)

                            def vh_(v, p2=p2, dsl=dsl, msub=msub, nh=nh, k_=k_):
                                c.w(v, "act", evu[(p2, k_)])
                                kc = dsl * 4 + msub
                                dst = (C[:, kc * 1024 + nh * 512:kc * 1024 + nh * 512 + 512] if kc < 16
                                       else Dv[:, (kc - 16) * 1024 + nh * 512:(kc - 16) * 1024 + nh * 512 + 512])
                                i = v.tensor_tensor(out=dst,
                                                    in0=Fb[:, (k_ % 2) * 512:(k_ % 2) * 512 + 512],
                                                    in1=Fb[:, 1024 + (k_ % 2) * 512:1024 + (k_ % 2) * 512 + 512],
                                                    op=ALU.mult)
                                hml[(p2, k_)] = c.inc("dve", i)
                            V_(vh_)

                for hf in range(2):
                    tt = p2 * 2 + hf
                    for kc in range(32):
                        dkey = (p2 * 2 + hf) * 32 + kc

                        def gwd(g, p2=p2, hf=hf, kc=kc, dkey=dkey):
                            if dkey >= 4:
                                c.w(g, "pe", mmm.get(dkey - 4, 0))
                            i = g.dma_start(out=Fb[:, 4096 + (kc % 4) * 1024:4096 + (kc % 4) * 1024 + 1024],
                                            in_=dt_["wd"][kc * 128:(kc + 1) * 128, :])
                            wldd[dkey] = (f"sd{kc % 4}", c.inc(f"sd{kc % 4}", i, 16))
                        G_(gwd)

                        def tm_(t, p2=p2, hf=hf, tt=tt, kc=kc, dkey=dkey):
                            c.w(t, wldd[dkey][0], wldd[dkey][1])
                            if kc == 0:
                                c.w(t, "dve", hml[(p2, 63)])
                                if hf == 0 and p2 == 0:
                                    pass
                                c.w(t, "act", evm.get((p2, hf - 1, 7), evm.get((p2 - 1, 1, 7), 0)) if hf or p2 else 0)
                            for dc in range(NCH):
                                ps_ = psA if dc < 4 else psB
                                i = t.matmul(ps_[:, (dc % 4) * 512:(dc % 4) * 512 + 512],
                                             Fb[:, 4096 + (kc % 4) * 1024 + dc * 128:4096 + (kc % 4) * 1024 + (dc + 1) * 128],
                                             C[:, kc * 1024 + hf * 512:kc * 1024 + hf * 512 + 512] if kc < 16
                                             else Dv[:, (kc - 16) * 1024 + hf * 512:(kc - 16) * 1024 + hf * 512 + 512],
                                             start=(kc == 0), stop=(kc == 31))
                            mmm[dkey] = c.inc("pe", i)
                        T_(tm_)

                    def sm_(s, p2=p2, hf=hf, tt=tt):
                        c.w(s, "pe", mmm[(p2 * 2 + hf) * 32 + 31])
                        for dc in range(NCH):
                            ps_ = psA if dc < 4 else psB
                            i = s.activation(out=A[:, dc * TOK + tt * 512:dc * TOK + (tt + 1) * 512],
                                             in_=ps_[:, (dc % 4) * 512:(dc % 4) * 512 + 512],
                                             func=AF.Identity, bias=0.0, scale=1.0)
                            assert c.inc("act", i) == evm[(p2, hf, dc)]
                    sw_sq.append(sm_)

            def swflush(s):
                for f in sw_sq:
                    f(s)
            S_(swflush)
            return c.cnt["act"], c.cnt["pe"], c.cnt["dve"]

        def emit_up(swiglu_done, grid_G_done):
            sw_act, sw_pe, sw_dve = swiglu_done

            def gup(g):
                c.w(g, "pe", sw_pe)
                for ec in range(32):
                    dst = C[:, ec * 1024:(ec + 1) * 1024] if ec < 16 else Dv[:, (ec - 16) * 1024:(ec - 15) * 1024]
                    i = g.dma_start(out=dst, in_=dt_["up_nat"][ec * 128:(ec + 1) * 128, :])
                    c.inc("upl", i, 16)
            G_(gup)
            UPL = c.cnt["upl"]

            gtl1 = {}
            gtl2 = {}
            mm1d = {}
            mm2d = {}
            mmup = {}
            cmb = {}
            g1ev = {}
            g1st = {}
            act0_u = c.cnt["act"]
            for i_ in range(32):
                g1ev[i_] = act0_u + i_ + 1
            up_sq = []

            def _lhs(ec, dc):
                return (C[:, ec * 1024 + dc * 128:ec * 1024 + (dc + 1) * 128] if ec < 16
                        else Dv[:, (ec - 16) * 1024 + dc * 128:(ec - 16) * 1024 + (dc + 1) * 128])

            def _rhs(ec):
                return (Eb[:, ec * 512:(ec + 1) * 512] if ec < 16
                        else Fb[:, (ec - 16) * 512:(ec - 15) * 512])

            def _bank(dc):
                ps_ = psA if dc < 4 else psB
                return ps_[:, (dc % 4) * 512:(dc % 4) * 512 + 512]

            for tt in range(NT4):
                def y1(sy, tt=tt):
                    if tt == 0:
                        c.w(sy, "gG", grid_G_done)
                        c.w(sy, "pe", sw_pe)
                        c.w(sy, "act", sw_act)
                    else:
                        c.w(sy, "pe", mm1d[tt - 1])
                    for ec in range(16):
                        i = sy.dma_start_transpose(out=_rhs(ec),
                                                   in_=dt_["g_dram"][tt * 512:(tt + 1) * 512, ec * 128:(ec + 1) * 128])
                        c.inc("gt1", i, 16)
                    gtl1[tt] = c.cnt["gt1"]

                    def y2(sy, tt=tt):
                        if tt >= 1:
                            c.w(sy, "pe", mm2d[tt - 1])
                        for ec in range(16, 32):
                            i = sy.dma_start_transpose(out=_rhs(ec),
                                                       in_=dt_["g_dram"][tt * 512:(tt + 1) * 512, ec * 128:(ec + 1) * 128])
                            c.inc("gt2", i, 16)
                        gtl2[tt] = c.cnt["gt2"]
                    return y2
                SY_(lambda sy, tt=tt, f=y1: f(sy)(sy))

                def t1(t, tt=tt):
                    if tt == 0:
                        c.w(t, "upl", UPL)
                        c.w(t, "act", sw_act)
                    c.w(t, "gt1", gtl1[tt])
                    if tt >= 1:
                        c.w(t, "dve", cmb[(tt - 1) * 8 + 7])
                    for dc in range(NCH):
                        for ec in range(16):
                            i = t.matmul(_bank(dc), _lhs(ec, dc), _rhs(ec),
                                         start=(ec == 0), stop=False)
                    mm1d[tt] = c.inc("pe", i)
                T_(t1)

                def t2(t, tt=tt):
                    c.w(t, "gt2", gtl2[tt])
                    for dc in range(NCH):
                        for ec in range(16, 32):
                            i = t.matmul(_bank(dc), _lhs(ec, dc), _rhs(ec),
                                         start=False, stop=(ec == 31))
                        mmup[tt * 8 + dc] = c.inc("pe", i)
                    mm2d[tt] = mmup[tt * 8 + 7]
                T_(t2)

                for dc in range(NCH):
                    ukey = tt * 8 + dc

                    def vu(v, tt=tt, dc=dc, ukey=ukey):
                        c.w(v, "pe", mmup[ukey])
                        if ukey == 0:
                            c.w(v, "act", sw_act)
                        if ukey >= 2:
                            c.w(v, "act", g1ev[ukey - 2])
                        i = v.scalar_tensor_tensor(out=SCR[:, (ukey % 2) * 512:(ukey % 2) * 512 + 512],
                                                   in0=_bank(dc),
                                                   scalar=CB[:, 57 + dc:58 + dc],
                                                   in1=A[:, dc * TOK + tt * 512:dc * TOK + (tt + 1) * 512],
                                                   op0=ALU.add, op1=ALU.add)
                        cmb[ukey] = c.inc("dve", i)
                    V_(vu)

                    def su(s, tt=tt, dc=dc, ukey=ukey):
                        c.w(s, "dve", cmb[ukey])
                        if ukey >= 4:
                            c.w(s, g1st[ukey - 4][0], g1st[ukey - 4][1])
                        i = s.activation(out=WB[:, (ukey % 4) * 512:(ukey % 4) * 512 + 512],
                                         in_=SCR[:, (ukey % 2) * 512:(ukey % 2) * 512 + 512],
                                         func=AF.Silu, bias=0.0, scale=1.0)
                        assert c.inc("act", i) == g1ev[ukey]
                    up_sq.append(su)

                    def yu(sy, tt=tt, dc=dc, ukey=ukey):
                        c.w(sy, "act", g1ev[ukey])
                        i = sy.dma_start(out=dt_["g_act"][dc * 128:(dc + 1) * 128, tt * 512:(tt + 1) * 512],
                                         in_=WB[:, (ukey % 4) * 512:(ukey % 4) * 512 + 512])
                        g1st[ukey] = (f"g1{ukey % 4}", c.inc(f"g1{ukey % 4}", i, 16))
                    SY_(yu)
            def uflush(s):
                for f in up_sq:
                    f(s)
            S_(uflush)
            st_deps = [(f"g1{j}", c.cnt.get(f"g1{j}", 0)) for j in range(4)]
            return st_deps, c.cnt["pe"], c.cnt["dve"], c.cnt["act"]

        PH = upto
        if PH >= 1:
            TERN0 = emit_l0_tern()
        ln0 = emit_ln(dt_["xT"], prev_done=[("dve", c.cnt["dve"] if PH >= 1 else 0)])
        if "t_xn" in taps:
            tap_fm("t_xn", Bb, [("dve", ln0)])
        if PH >= 1:
            ffn0 = emit_l0_ffn_mm(ln0, TERN0)
        if PH >= 2:
            qkv_prev = [("pe", ffn0[1]), ("act", ffn0[2])] if PH >= 1 else [("dve", ln0)]
            qkv0 = emit_qkv(0, ln0, qkv_prev)
            if "t_qn" in taps:
                tap_fm("t_qn", A, [("dve", qkv0[0]), ("act", qkv0[1])])
            if "t_kn" in taps:
                tap_fm("t_kn", C, [("dve", qkv0[0]), ("act", qkv0[1])])
            if "t_v" in taps:
                tapout("t_v", Dv[:], deps=[("act", qkv0[1]), ("dve", qkv0[0])])
        if PH >= 3:
            at0 = emit_attn(0, [("dve", qkv0[0]), ("act", qkv0[1]), ("gst", c.cnt["gst"])])
            if "t_attn" in taps:
                tap_fm("t_attn", A, [("dve", at0)])
        if PH >= 4:
            pr0 = emit_proj_residual(0, at0)

        if PH >= 5:
            _allsems = [(nm, c.cnt[nm]) for nm in sorted(c.cnt.keys()) if nm not in ("pe", "act", "dve", "pool")]
            ln1 = emit_ln(dt_["x_mid"], prev_done=_allsems)
            if "t_xn1" in taps:
                tap_fm("t_xn1", Bb, [("dve", ln1)])
        if PH >= 6:
            qpk = emit_qpk(ln1, [("dve", ln1)])
            if "t_qpk" in taps:
                def _tq(g):
                    c.w(g, "act", qpk)
                    i = g.dma_start(out=dt_["t_qpk"].rearrange("(h p) t -> p h t", p=128),
                                    in_=Eb[:].rearrange("p (h t) -> p h t", h=4))
                    c.inc("gst", i, 16)
                G_(_tq)
        if PH >= 7:
            wf_st, ktp_ld, wf_act = emit_wfull(ln1)
            lst = emit_lists(qpk)
            if "t_tau" in taps:
                def _tt(g):
                    c.w(g, "dve", lst)
                    i = g.dma_start(out=dt_["t_tau"], in_=SM[:, 560:624])
                    c.inc("gst", i, 16)
                G_(_tt)
            if "t_negc" in taps:
                def _tn(g):
                    c.w(g, "dve", lst)
                    i = g.dma_start(out=dt_["t_negc"], in_=SM[:, 624:688])
                    c.inc("gst", i, 16)
                G_(_tn)
        if PH >= 8:
            grid_G = emit_grid(wf_st, ktp_ld, wf_act, lst)
        if PH >= 9:
            swg = emit_swiglu(ln1, [("dve", c.cnt["dve"]), ("act", c.cnt["act"]), ("gG", c.cnt.get("gG", 0)), ("gst", c.cnt["gst"]), ("pool", c.cnt["pool"])])
            if "t_mlp" in taps:
                tap_fm("t_mlp", A, [("act", swg[0])])
        if PH >= 10:
            upres = emit_up(swg, grid_G)
        if PH >= 11:
            prev1 = [("pe", upres[1]), ("dve", upres[2]), ("gst", c.cnt["gst"])] + upres[0]
            qkv1 = emit_qkv(1, ln1, prev1)
        if PH >= 12:
            at1 = emit_attn(1, [("dve", qkv1[0]), ("act", qkv1[1]), ("gst", c.cnt["gst"])])
            if "t_attn1" in taps:
                tap_fm("t_attn1", A, [("dve", at1)])
        if PH >= 13:
            pr1 = emit_proj_residual(1, at1, extra_deps=upres[0])

        def fin(g):
            for nm in sorted(c.cnt.keys()):
                c.w(g, nm)
        G_(fin)

    return nc, dt_



def _prep_core_inputs(core, full):
    f32 = np.float32
    out = {"xT": np.ascontiguousarray(full["L0_attn_x"][core].T, dtype=f32)}
    out["l0_wqkvT"] = np.ascontiguousarray(full["L0_attn_Wqkv"].T, dtype=f32)
    out["l0_bqkv_col"] = np.ascontiguousarray(full["L0_attn_bqkv"].reshape(24, 128).T, dtype=f32)
    out["l0_bqkv_vrow"] = np.ascontiguousarray(full["L0_attn_bqkv"][2048:3072].reshape(1, 1024), dtype=f32)
    out["l0_wprojT"] = np.ascontiguousarray(full["L0_attn_Wproj"].T, dtype=f32)
    out["l0_bproj_col"] = np.ascontiguousarray(full["L0_attn_bproj"].reshape(8, 128).T, dtype=f32)
    out["qpat_col"] = np.ascontiguousarray(np.tile(np.asarray(full["L0_attn_qn_w"]), 16).reshape(8, 128).T, dtype=f32)
    out["kpat_col"] = np.ascontiguousarray(np.tile(np.asarray(full["L0_attn_kn_w"]), 16).reshape(8, 128).T, dtype=f32)
    out["l0_ffnT"] = np.ascontiguousarray(full["L0_ffn_weight"].T, dtype=f32)
    out["alpha_col"] = np.ascontiguousarray(np.broadcast_to(np.asarray(full["L0_ffn_alpha"]).reshape(1, 1), (128, 1)), dtype=f32)
    out["l0_fbias_col"] = np.ascontiguousarray(np.asarray(full["L0_ffn_bias"]).reshape(8, 128).T, dtype=f32)
    k1 = np.asarray(full["L1_attn_kernel"])
    out["l1_wqkvT"] = np.ascontiguousarray(np.transpose(k1, (2, 0, 1, 3)).reshape(1024, 3072), dtype=f32)
    wo = np.asarray(full["L1_attn_WO"])[0]
    out["l1_woT"] = np.ascontiguousarray(wo.reshape(64, 16, 1024).transpose(1, 0, 2).reshape(1024, 1024), dtype=f32)
    wq = np.asarray(full["L1_ffn_Wq"])
    out["wq_p"] = np.ascontiguousarray(wq.reshape(1024, 2, 4, 64).transpose(0, 2, 1, 3).reshape(1024, 512), dtype=f32)
    keys = np.asarray(full["L1_ffn_keys"])
    out["keysP"] = np.ascontiguousarray(keys.transpose(0, 2, 3, 1).reshape(512, 64), dtype=f32)
    kx = keys[:, :, 0, :]
    ky = keys[:, :, 1, :]
    ktp = np.empty((4, 128, NEXP), dtype=f32)
    for h in range(4):
        ktp[h, 0:64] = np.repeat(kx[h].T, 64, axis=1)
        ktp[h, 64:128] = np.tile(ky[h].T, (1, 64))
    out["ktp"] = np.ascontiguousarray(ktp.reshape(512, NEXP))
    out["down_T"] = np.ascontiguousarray(np.asarray(full["L1_ffn_down_embed"]).T, dtype=f32)
    out["up_nat"] = np.ascontiguousarray(np.asarray(full["L1_ffn_up_embed"]), dtype=f32)
    out["wg"] = np.ascontiguousarray(np.asarray(full["L1_ffn_Wg"]), dtype=f32)
    out["wu"] = np.ascontiguousarray(np.asarray(full["L1_ffn_Wu"]), dtype=f32)
    out["wd"] = np.ascontiguousarray(np.asarray(full["L1_ffn_Wd"]), dtype=f32)
    out["bg_col"] = np.ascontiguousarray(np.asarray(full["L1_ffn_bg"]).reshape(32, 128).T, dtype=f32)
    out["bu_col"] = np.ascontiguousarray(np.asarray(full["L1_ffn_bu"]).reshape(32, 128).T, dtype=f32)
    out["bd_col"] = np.ascontiguousarray(np.asarray(full["L1_ffn_bd"]).reshape(8, 128).T, dtype=f32)
    blk2 = np.zeros((128, 2), dtype=f32)
    blk2[0:64, 0] = 1.0
    blk2[64:128, 1] = 1.0
    out["blk2"] = blk2
    epscol = np.zeros((128, 4), dtype=f32)
    epscol[:, 0] = LN_EPS
    epscol[:, 1] = RMS_EPS
    epscol[:, 2] = 1e-30
    out["epscol"] = epscol
    sel2 = np.zeros((2, 128), dtype=f32)
    sel2[0, 0:64] = 1.0
    sel2[1, 64:128] = 1.0
    out["sel2"] = sel2
    return out


def kernel(**inputs):
    nc, dt_ = build_program()
    in_maps = [_prep_core_inputs(core, inputs) for core in range(8)]
    res = run_bass_kernel_spmd(nc, in_maps, core_ids=list(range(8)))
    outs = [np.asarray(r["outT"], dtype=np.float32).T for r in res.results]
    return np.stack(outs, axis=0)
